# version 05 (final)
# baseline (speedup 1.0000x reference)
import numpy as np
import ml_dtypes
import concourse.bass as bass
import concourse.mybir as mybir
from concourse.bass_utils import run_bass_kernel_spmd
from concourse.library_config import mlp as LIB_MLP, sparse_gather as LIB_SG
from concourse.library_overlay import lower_extended_insts
from contextlib import ExitStack

F32 = mybir.dt.float32
BF16 = mybir.dt.bfloat16
I16 = mybir.dt.int16
U32 = mybir.dt.uint32
AF = mybir.ActivationFunctionType
ALU = mybir.AluOpType
AX = mybir.AxisListType
PI = float(np.pi)

SEQ, D, NH, DH, DR = 4096, 1024, 8, 64, 32
NIH, KTS, LW, NL = 4, 96, 64, 32
NT, NS = 32, 16
INTER, E, CAP = 2048, 8, 768
NCT = CAP // 128
SCALE = float((DH + DR) ** -0.5)
KVROW, MUROW, VROW, XNROW = 1536, 512, 1024, 2048
NEG = -1.0e38
ENGS = ["sync", "scalar", "vector", "tensor", "gpsimd"]
BF = ml_dtypes.bfloat16


_PREP_CACHE = {}


def _cached(name, fn):
    if name not in _PREP_CACHE:
        _PREP_CACHE[name] = fn()
    return _PREP_CACHE[name]


def prep_inputs(inputs, core):
    b, h = core // 2, core % 2
    x = np.asarray(inputs["L0_attn_x"])
    theta = (1.0 / 10000.0 ** (2.0 * np.arange(16, dtype=np.float64) / DR))
    tiles = [2 * s + h for s in range(NS)]
    rows = np.concatenate([np.arange(128 * T, 128 * T + 128) for T in tiles])

    m = {}
    m["x_seq"] = _cached("x_seq%d" % b, lambda: np.ascontiguousarray(x[b]))
    m["x_own"] = np.ascontiguousarray(x[b][rows])
    def _flat_cn(ws, nch):
        width = sum(w.shape[1] for w in ws)
        o = np.empty((128, nch * width), np.float32)
        for c in range(nch):
            col = 0
            for w in ws:
                n = w.shape[1]
                o[:, c * width + col: c * width + col + n] = w[c * 128:(c + 1) * 128, :]
                col += n
        return o

    m["wseq_f"] = _cached("wseq_f", lambda: _flat_cn(
        [inputs["L0_attn_W_dkv"], inputs["L0_attn_idx_wk"], inputs["L0_attn_W_kr"]], 8))
    m["wkv_f"] = _cached("wkv_f", lambda: _flat_cn(
        [inputs["L0_attn_W_uk"], inputs["L0_attn_W_uv"]], 2))
    m["wown_f"] = _cached("wown_f", lambda: _flat_cn(
        [inputs["L0_attn_W_dq"], inputs["L0_attn_idx_wq"], np.ascontiguousarray(inputs["L0_ffn_gate_w"].T)], 8))
    m["wq2_f"] = _cached("wq2_f", lambda: _flat_cn(
        [inputs["L0_attn_W_uq"], inputs["L0_attn_W_qr"]], 4))
    m["wout_f"] = _cached("wout_f", lambda: _flat_cn([inputs["L0_attn_W_out"]], 4))
    m["idx_w4"] = np.ascontiguousarray(inputs["L0_attn_idx_w"]).reshape(1, 4)
    E_, INTER_, D_ = 8, 2048, 1024
    m["w1f"] = _cached("w1f", lambda: np.ascontiguousarray(
        np.transpose(inputs["L0_ffn_w1"], (0, 2, 1)).reshape(E_, 8, 128, 4, 512)
        .transpose(0, 2, 3, 1, 4).reshape(E_, 128, 16384)))
    m["w3f"] = _cached("w3f", lambda: np.ascontiguousarray(
        np.transpose(inputs["L0_ffn_w3"], (0, 2, 1)).reshape(E_, 8, 128, 4, 512)
        .transpose(0, 2, 3, 1, 4).reshape(E_, 128, 16384)))
    m["w2f"] = _cached("w2f", lambda: np.ascontiguousarray(
        np.transpose(inputs["L0_ffn_w2"], (0, 2, 1)).reshape(E_, 16, 128, 1024)
        .transpose(0, 2, 1, 3).reshape(E_, 128, 16384)))
    m["sw1f"] = _cached("sw1f", lambda: np.ascontiguousarray(
        inputs["L0_ffn_sw1"].T.reshape(8, 128, 4, 512).transpose(1, 2, 0, 3).reshape(128, 16384)))
    m["sw3f"] = _cached("sw3f", lambda: np.ascontiguousarray(
        inputs["L0_ffn_sw3"].T.reshape(8, 128, 4, 512).transpose(1, 2, 0, 3).reshape(128, 16384)))
    m["sw2f"] = _cached("sw2f", lambda: np.ascontiguousarray(
        inputs["L0_ffn_sw2"].T.reshape(16, 128, 1024).transpose(1, 0, 2).reshape(128, 16384)))
    m["raw_delta"] = np.ascontiguousarray(inputs["L0_attn_raw_delta"]).reshape(1, 16)
    m["gate_b128"] = np.broadcast_to(inputs["L0_ffn_gate_bias"], (128, E)).copy()

    tpos = rows.astype(np.float64)[:, None]
    m["tq_cos"] = np.cos(tpos * theta[None, :]).astype(np.float32)
    m["tq_sin"] = np.sin(tpos * theta[None, :]).astype(np.float32)
    kk = np.arange(SEQ, dtype=np.float64)[:, None]
    m["tk63_cos"] = np.cos((kk + 63) * theta[None, :]).astype(np.float32)
    m["tk63_sin"] = np.sin((kk + 63) * theta[None, :]).astype(np.float32)
    m["tk0_cos"] = np.cos(kk * theta[None, :]).astype(np.float32)
    m["tk0_sin"] = np.sin(kk * theta[None, :]).astype(np.float32)
    jj = np.arange(NL, dtype=np.float64)[None, :]
    ang = (64 + jj) * theta[:16, None] if False else (64 + jj) * theta[:, None]
    slc = np.zeros((32, 32), np.float32)
    sls = np.zeros((32, 32), np.float32)
    slc[0:16] = np.cos(ang).astype(np.float32)
    slc[16:32] = np.cos(ang).astype(np.float32)
    sls[0:16] = np.sin(ang).astype(np.float32)
    sls[16:32] = np.sin(ang).astype(np.float32)
    m["sl_cos"] = slc
    m["sl_sin"] = sls
    karr = np.arange(SEQ, dtype=np.float64)[None, :]
    f2 = np.zeros((NS * 128, SEQ), np.float32)
    tarr = tpos
    fut = karr > tarr
    loc = (karr >= tarr - 63) & (karr <= tarr)
    fill = (-5.0e37 - (karr + 1.0) * 1.0e30)
    f2[:] = np.where(loc, NEG, np.where(fut, fill, 0.0)).astype(np.float32)
    m["f2pack"] = f2
    bidx = np.zeros((NS * 128, 16), np.int16)
    for si, T in enumerate(tiles):
        t0 = 128 * T
        keys = np.clip(t0 - 63 + np.arange(256), 0, SEQ - 1).astype(np.int16)
        lay = np.zeros((128, 16), np.int16)
        for i, kv in enumerate(keys):
            lay[i % 16, i // 16] = kv
            lay[16 + i % 16, i // 16] = kv
        bidx[128 * si:128 * si + 128] = lay
    m["band_idx"] = bidx
    m["edge_mask"] = (tpos <= 62).astype(np.float32)
    m["nlfix128"] = np.broadcast_to(64.0 + np.arange(NL, dtype=np.float32), (128, NL)).copy()
    m["eio128"] = np.broadcast_to(np.arange(E, dtype=np.float32), (128, E)).copy()
    rp = np.zeros((NS * 128, 8), np.int16)
    for si, T in enumerate(tiles):
        t0 = 128 * T
        for i in range(128):
            rp[128 * si + i % 16, i // 16] = t0 + i
            rp[128 * si + 16 + i % 16, i // 16] = t0 + i
    m["rowpos_i16"] = rp
    cs = np.zeros((16, 128), np.float32)
    for l in range(2048):
        cs[l % 16, l // 16] = l
    m["cseqid"] = cs
    oi = np.zeros((128, NS * 8), np.int16)
    for ct in range(NS):
        for i in range(128):
            oi[i % 16, ct * 8 + i // 16] = ct * 128 + i
            oi[16 + i % 16, ct * 8 + i // 16] = ct * 128 + i
    m["ownid_i16"] = oi
    m["ident128"] = np.eye(128, dtype=np.float32)
    b2 = np.zeros((NS * 128, 16), np.int16)
    for si, T in enumerate(tiles):
        t0 = 128 * T
        keys = np.clip(t0 + 1 + np.arange(128), 0, SEQ - 1).astype(np.int16)
        lay = np.zeros((128, 16), np.int16)
        for i, kv in enumerate(keys):
            lay[i % 16, i // 16] = kv
            lay[16 + i % 16, i // 16] = kv
        b2[128 * si:128 * si + 128] = lay
    m["band_idx2"] = b2
    ei = np.zeros((128, 8), np.int16)
    for i in range(64):
        ei[i % 16, i // 16] = i
        ei[16 + i % 16, i // 16] = i
    m["eidx_i16"] = ei
    m["m16"] = ((np.arange(128) % 32) >= 16).astype(np.uint32).reshape(128, 1)
    sio = np.zeros((16, NCT * 8), np.float32)
    for i in range(CAP):
        sio[i % 16, i // 16] = i
    m["slot_iota"] = sio
    return m


def ref_outputs_shape():
    return (4, SEQ, D)


def build_nc(dump=None):
    nc = bass.Bass("TRN2", target_bir_lowering=False)
    stack = ExitStack()
    C = {}

    def dr(name, shape, dt=F32, kind="Internal"):
        t = nc.dram_tensor(name, shape, dt, kind=kind)
        C[name + "_t"] = t
        C[name] = t.ap()
        return C[name]

    APc = None
    IN_NAMES = ["x_seq", "x_own", "wseq_f", "wkv_f", "wown_f", "wq2_f", "wout_f",
                "idx_w4", "raw_delta",
                "gate_b128", "w1f", "w2f", "w3f", "sw1f", "sw2f", "sw3f",
                "tq_cos", "tq_sin", "tk63_cos", "tk63_sin", "tk0_cos", "tk0_sin",
                "sl_cos", "sl_sin", "f2pack", "band_idx", "edge_mask", "nlfix128",
                "eio128", "rowpos_i16", "cseqid", "ownid_i16", "slot_iota", "ident128", "band_idx2", "eidx_i16", "m16"]
    shapes = {
        "x_seq": ([SEQ, D], F32), "x_own": ([NS * 128, D], F32),
        "wseq_f": ([128, 8 * 352], F32), "wkv_f": ([128, 2 * 1024], F32),
        "wown_f": ([128, 8 * 776], F32), "wq2_f": ([128, 4 * 768], F32),
        "wout_f": ([128, 4 * 1024], F32),
        "idx_w4": ([1, 4], F32),
        "raw_delta": ([1, 16], F32),
        "gate_b128": ([128, E], F32),
        "w1f": ([E, 128, 16384], F32), "w2f": ([E, 128, 16384], F32), "w3f": ([E, 128, 16384], F32),
        "sw1f": ([128, 16384], F32), "sw2f": ([128, 16384], F32), "sw3f": ([128, 16384], F32),
        "tq_cos": ([NS * 128, 16], F32), "tq_sin": ([NS * 128, 16], F32),
        "tk63_cos": ([SEQ, 16], F32), "tk63_sin": ([SEQ, 16], F32),
        "tk0_cos": ([SEQ, 16], F32), "tk0_sin": ([SEQ, 16], F32),
        "sl_cos": ([32, 32], F32), "sl_sin": ([32, 32], F32),
        "f2pack": ([NS * 128, SEQ], F32),
        "band_idx": ([NS * 128, 16], I16),
        "edge_mask": ([NS * 128, 1], F32),
        "nlfix128": ([128, NL], F32), "eio128": ([128, E], F32),
        "rowpos_i16": ([NS * 128, 8], I16),
        "cseqid": ([16, 128], F32),
        "ownid_i16": ([128, NS * 8], I16),
        "slot_iota": ([16, NCT * 8], F32),
        "ident128": ([128, 128], F32),
        "band_idx2": ([NS * 128, 16], I16),
        "eidx_i16": ([128, 8], I16),
        "m16": ([128, 1], U32),
    }
    for n in IN_NAMES:
        sh, dt = shapes[n]
        dr(n, sh, dt, kind="ExternalInput")
    dr("out", [NS * 128, D], F32, kind="ExternalOutput")
    dr("xn2_d", [NS * 128, XNROW], BF16)

    dr("xn_d", [SEQ, XNROW], BF16)
    dr("kv_d", [SEQ, KVROW], BF16)
    dr("v_d", [SEQ, VROW], BF16)
    dr("mu_d", [SEQ, MUROW], BF16)
    dr("dtr_d", [64], F32)
    dr("gate_d", [NS * 128, 4], F32)
    dr("s_flat", [128 * 4096 + 8192], F32)
    dr("b_flat", [128 * 1536 + 4096], F32)
    dr("e_flat", [128 * 768], F32)
    dr("aidx_d", [128 * NL], I16)
    dr("aidx2_d", [128 * NL + 64], I16)
    dr("a_flat", [2 * 128 * 768 + 4096], BF16)
    dr("ab_flat", [2 * 128 * 8 * 192 + 8192], BF16)
    dr("ae_flat", [2 * 128 * 8 * 64 + 4096], BF16)
    dr("ci_d", [E * CAP + 256], I16)
    dr("av_flat", [128 * 512 + 1024], F32)
    dr("anl_d", [2 * 4 * 8192 + 64], BF16)
    dr("rot_d", [2048], F32)
    dr("cw_d", [E * CAP + 256], F32)
    dbg = dump is not None
    dr("dbg", [128, 4096], F32, kind="ExternalOutput" if dbg else "Internal")

    TL = {}
    def sb(name, shape, dt=F32):
        TL[name] = stack.enter_context(nc.sbuf_tensor(name, shape, dt))
        return TL[name]

    sb("wseq_h", [128, 8 * 352], BF16); sb("wseq_l", [128, 8 * 352], BF16)
    sb("wkv_h", [128, 2 * 1024], BF16); sb("wkv_l", [128, 2 * 1024], BF16)
    sb("AR0", [128, 6208])
    sb("AR1", [128, 4096])
    sb("AR2", [128, 4096])
    sb("ARB", [128, 24576], BF16)
    sb("ones_row", [1, 128])
    sb("rowA", [1, 16]); sb("rowB", [1, 16]); sb("rowC", [1, 16]); sb("rowD", [1, 16])
    sb("sind128", [128, 16]); sb("cosd128", [128, 16])
    sb("sindP", [16, 1]); sb("cosdP", [16, 1])
    sb("slc", [32, 32]); sb("sls", [32, 32])
    sb("rr1", [16, 32]); sb("rr2", [16, 32]); sb("rr3", [16, 32]); sb("rr4", [16, 32])
    sb("rT0", [128, 32]); sb("rT1", [128, 32])
    sb("rT0r", [128, 512]); sb("rT1r", [128, 512])
    sb("ident128s", [128, 128])
    sb("gb128", [128, 8]); sb("eio", [128, 8]); sb("nlf", [128, NL]); sb("idxw", [128, 4])
    sb("stat", [128, 8]); sb("epst", [128, 1])
    sb("xh", [128, D], BF16); sb("xl", [128, D], BF16)
    sb("xnT_h", [128, 1024], BF16); sb("xnT_l", [128, 1024], BF16)
    sb("kiT_h", [64, SEQ], BF16); sb("kiT_l", [64, SEQ], BF16)
    sb("ckvh", [128, 256], BF16); sb("ckvl", [128, 256], BF16)
    sb("ckvT_h", [128, 256], BF16); sb("ckvT_l", [128, 256], BF16)
    sb("m32a", [128, 32]); sb("m32b", [128, 32]); sb("m32c", [128, 32]); sb("m32d", [128, 32])
    sb("kvrow", [128, KVROW], BF16); sb("vrow", [128, VROW], BF16); sb("murow", [128, MUROW], BF16)
    sb("t16a", [128, 16]); sb("t16b", [128, 16]); sb("t16c", [128, 16]); sb("t16d", [128, 16])
    sb("qsp", [128, 256])
    sb("qmh", [128, 1024], BF16); sb("qml", [128, 1024], BF16)
    sb("qT_h", [128, 1024], BF16); sb("qT_l", [128, 1024], BF16)
    sb("qiT_h", [64, 512], BF16); sb("qiT_l", [64, 512], BF16)
    sb("cqh", [128, 512], BF16); sb("cql", [128, 512], BF16)
    sb("cqT_h", [128, 512], BF16); sb("cqT_l", [128, 512], BF16)
    sb("gidx", [128, 8], U32); sb("g4", [128, 4])
    sb("m16s", [128, 1], U32)
    sb("emu", [128, 1], U32)
    sb("ivldu", [16, NCT * 8], U32)
    sb("selv", [128, NL]); sb("seli", [128, NL], U32); sb("self32", [128, NL])
    sb("seli16", [128, NL], I16)
    sb("gidx16", [128, 512], I16)
    sb("seli16s", [128, 32], I16)
    sb("bidx", [128, 16], I16); sb("bidx2", [128, 16], I16); sb("eidx", [128, 8], I16)
    sb("mrot", [128, 512])
    sb("sc", [128, 768])
    sb("sfm", [128, 8]); sb("sfs", [128, 8]); sb("sfr", [128, 8]); sb("sfmn", [128, 8])
    sb("oh", [128, 512], BF16); sb("ol", [128, 512], BF16)
    sb("oT_h", [128, 512], BF16); sb("oT_l", [128, 512], BF16)
    sb("xo", [128, D])
    sb("rp", [128, 8], I16); sb("em", [128, 1])
    sb("aband_h", [128, 64], BF16); sb("aband_l", [128, 64], BF16)
    sb("abT", [128, 128], BF16)
    sb("aedge_h", [128, 64], BF16); sb("aedge_l", [128, 64], BF16)
    sb("gA", [16, 128]); sb("gB", [16, 128]); sb("gC", [16, 128]); sb("gD", [16, 128])
    sb("gE", [16, 128]); sb("gF", [16, 128]); sb("gG", [16, 128])
    sb("csq", [16, 128])
    sb("sga", [16, NCT * 8]); sb("sgw", [16, NCT * 8]); sb("sgn", [1, 1], U32)
    sb("nff", [16, 1]); sb("nffrow", [1, 1])
    sb("vld", [16, NCT * 8]); sb("ivld", [16, NCT * 8]); sb("slio", [16, NCT * 8])
    sb("z16", [16, NCT * 8]); sb("z128", [128, 8])
    sb("sgi16", [16, NCT * 8], I16)
    sb("fidx", [128, NCT * 8], I16); sb("oidx", [128, NS * 8], I16)
    sb("ew", [128, 1])
    sb("hh", [128, 512], BF16); sb("hl", [128, 512], BF16)

    A = TL["ARB"][:]
    def bslice(off, n):
        return A[:, off:off + n]
    CV = {}
    CV["kg"] = bslice(0, 6144)
    CV["bg"] = bslice(6144, 3072)
    CV["mg"] = bslice(9216, 2048)
    CV["vg"] = bslice(11264, 4096)
    CV["vband"] = bslice(15360, 3072)
    CV["mhh"] = bslice(18432, 512); CV["mhl"] = bslice(18944, 512)
    CV["a_h"] = bslice(19456, 768); CV["a_l"] = bslice(20224, 768)
    CV["ablk_h"] = bslice(20992, 1024); CV["ablk_l"] = bslice(22016, 1024)
    CV["abst"] = bslice(23040, 512)
    CV["wown_h"] = bslice(0, 6208); CV["wown_l"] = bslice(6208, 6208)
    CV["wq2_h"] = bslice(12416, 3072); CV["wq2_l"] = bslice(15488, 3072)
    CV["wout_h"] = bslice(0, 4096); CV["wout_l"] = bslice(4096, 4096)
    CV["f_w1h"] = bslice(0, 4096); CV["f_w1l"] = bslice(4096, 4096)
    CV["f_w3h"] = bslice(8192, 4096); CV["f_w3l"] = bslice(12288, 4096)
    CV["f_w2h"] = bslice(16384, 4096); CV["f_w2l"] = bslice(20480, 4096)
    CV["S"] = TL["AR0"][:, 0:4096]
    CV["fstage"] = TL["AR0"][:, 0:6208]
    CV["y"] = TL["AR0"][:]
    CV["s_all"] = TL["AR1"][:]
    CV["f_h"] = TL["AR1"][:, 0:2048]
    CV["band_all"] = TL["AR2"][:, 0:1536]
    CV["edge_all"] = TL["AR2"][:, 1536:2304]
    CV["o"] = TL["AR2"][:, 2304:2816]
    CV["rtmp"] = TL["AR2"][:, 2816:3328]
    CV["rtmp2"] = TL["AR2"][:, 3328:3840]
    CV["base"] = TL["AR2"][:, 0:1024]
    CV["sc2"] = TL["AR2"][:, 1024:1792]
    C["CV"] = CV
    P1 = stack.enter_context(nc.psum_tensor("P1", [128, 2048], F32))
    P2 = stack.enter_context(nc.psum_tensor("P2", [128, 2048], F32))

    sems = {n: stack.enter_context(nc.semaphore(n)) for n in
            ["ve", "ac", "te", "gp", "dmo", "bar"]}
    sems["dma"] = [stack.enter_context(nc.semaphore(f"dma{i}")) for i in range(24)]
    sems["dmb"] = [stack.enter_context(nc.semaphore(f"dmb{i}")) for i in range(8)]
    sems["dmc"] = [stack.enter_context(nc.semaphore(f"dmc{i}")) for i in range(16)]

    em = Emitter(nc, C, TL, CV, P1, P2, sems)
    em.emit_all(dump)

    import contextlib
    with nc.allow_non_contiguous_dma(reason="small relayout/bounce reads"), nc.Block() as block:
        @block.sync
        def _(e):
            for f in em.ops["sync"]:
                f(e)
        @block.scalar
        def _(e):
            for f in em.ops["scalar"]:
                f(e)
        @block.vector
        def _(e):
            for f in em.ops["vector"]:
                f(e)
        @block.tensor
        def _(e):
            for f in em.ops["tensor"]:
                f(e)
        @block.gpsimd
        def _(e):
            for f in em.ops["gpsimd"]:
                f(e)

    lower_extended_insts(nc)
    C["_stack"] = stack
    return nc, IN_NAMES, C


class Emitter:
    def __init__(s, nc, C, TL, CV, P1, P2, sems):
        s.nc, s.C, s.TL, s.CV, s.P1, s.P2 = nc, C, TL, CV, P1, P2
        s.sems = sems
        s.ledg = {k: 0 for k in ["ve", "ac", "te", "gp", "dmo", "bar"]}
        s.dled = {k: [0] * len(sems[k]) for k in ["dma", "dmb", "dmc"]}
        s.drot = {k: 0 for k in ["dma", "dmb", "dmc"]}
        s.ops = {k: [] for k in ENGS}
        s.barv = 0
        s.AP = type(C["x_seq"])
        s.regs = {}

    def q(s, eng, fn):
        s.ops[eng].append(fn)

    def _dtok(s, key):
        i = s.drot[key] % len(s.sems[key])
        s.drot[key] += 1
        s.dled[key][i] += 16
        return (key, i, s.dled[key][i])

    def dma(s, eng, out, in_, **kw):
        key = {"sync": "dma", "scalar": "dmb", "gpsimd": "dmc"}[eng]
        tok = s._dtok(key)
        h = s.sems[key][tok[1]]
        s.q(eng, lambda e: e.dma_start(out=out, in_=in_, **kw).then_inc(h, 16))
        return tok

    def dmaT(s, eng, out, in_):
        key = {"sync": "dma", "scalar": "dmb"}[eng]
        tok = s._dtok(key)
        h = s.sems[key][tok[1]]
        s.q(eng, lambda e: e.dma_start(out=out, in_=in_, transpose=True).then_inc(h, 16))
        return tok

    def gdma(s, fn):
        tok = s._dtok("dmc")
        h = s.sems["dmc"][tok[1]]
        s.q("gpsimd", lambda e: fn(e, h))
        return tok

    def wait(s, eng, tok):
        if tok is None:
            return
        if isinstance(tok[0], str):
            key, i, v = tok
            assert s.dled[key][i] == v, f"stale dma token {key}[{i}]: {v} != {s.dled[key][i]}"
            h = s.sems[key][i]
        else:
            h, v = tok
        s.q(eng, lambda e: e.wait_ge(h, v))

    def sig(s, eng):
        key = {"vector": "ve", "scalar": "ac", "tensor": "te", "gpsimd": "gp", "sync": "dmo"}[eng]
        h = s.sems[key]
        s.ledg[key] += 1
        v = s.ledg[key]
        if eng == "vector":
            s.q(eng, lambda e: e.drain())
            s.q(eng, lambda e: e.engine_nop().then_inc(h, 1))
        elif eng == "sync":
            s.q(eng, lambda e: e.nop().then_inc(h, 1))
        else:
            s.q(eng, lambda e: e.drain())
            s.q(eng, lambda e: e.nop().then_inc(h, 1))
        return (h, v)

    def barrier(s):
        s.barv += 5
        tgt = s.barv
        B = s.sems["bar"]
        qown = {"dma": "sync", "dmb": "scalar", "dmc": "gpsimd"}
        for key, eng in qown.items():
            for i, v in enumerate(s.dled[key]):
                if v > 0:
                    h = s.sems[key][i]
                    s.q(eng, lambda e, h=h, v=v: e.wait_ge(h, v))
        for eng in ENGS:
            if eng == "vector":
                s.q(eng, lambda e: e.drain())
                s.q(eng, lambda e: e.engine_nop().then_inc(B, 1))
            elif eng == "sync":
                s.q(eng, lambda e: e.nop().then_inc(B, 1))
            else:
                s.q(eng, lambda e: e.drain())
                s.q(eng, lambda e: e.nop().then_inc(B, 1))
        for eng in ENGS:
            s.q(eng, lambda e, tgt=tgt: e.wait_ge(B, tgt))

    def v(s, fn):
        s.q("vector", lambda e: e.drain())
        s.q("vector", fn)

    def a(s, fn):
        s.q("scalar", lambda e: e.drain())
        s.q("scalar", fn)

    def t(s, fn):
        s.q("tensor", fn)

    def g(s, fn):
        s.q("gpsimd", lambda e: e.drain())
        s.q("gpsimd", fn)

    def vd(s):
        pass

    def mm3(s, out, lh, ll, rh, rl, start, stop, tp=None):
        kw = {"tile_position": tp} if tp is not None else {}
        s.t(lambda e: e.matmul(out, lh, rh, start=start, stop=False, **kw))
        s.t(lambda e: e.matmul(out, lh, rl, start=False, stop=False, **kw))
        s.t(lambda e: e.matmul(out, ll, rh, start=False, stop=stop, **kw))

    def split(s, x, hi, lo):
        s.v(lambda e: e.tensor_copy(out=hi, in_=x))
        s.vd()
        s.v(lambda e: e.tensor_sub(lo, x, hi))
        s.vd()

    def rap(s, tname, offset, dims):
        return s.AP(tensor=s.C[tname + "_t"], offset=offset, ap=[list(d) for d in dims])

    def p0(s):
        T, CV, C = s.TL, s.CV, s.C
        s.v(lambda e: e.memset(T["ones_row"][:], 1.0))
        s.v(lambda e: e.memset(T["epst"][:], 1e-5))
        s.v(lambda e: e.memset(T["stat"][:], 0.0))
        s.v(lambda e: e.memset(T["z16"][:], 0.0))
        s.v(lambda e: e.memset(T["z128"][:], 0.0))
        s.v(lambda e: e.memset(T["ARB"][:], 0.0))
        s.vd()
        ve0 = s.sig("vector")
        s.wait("sync", ve0)
        zsrc = T["ARB"][:]
        self_toks = []
        self_toks.append(s.dma("sync", s.rap("ab_flat", 0, [[2048, 128], [1, 2048]]), zsrc[:, 0:2048]))
        self_toks.append(s.dma("sync", s.rap("ab_flat", 2048 * 128, [[1088, 128], [1, 1088]]), zsrc[:, 0:1088]))
        self_toks.append(s.dma("sync", s.rap("ae_flat", 0, [[1056, 128], [1, 1056]]), zsrc[:, 0:1056]))
        s.c_abzero = self_toks
        tok = s.dma("sync", T["rowA"][:], C["raw_delta"])
        s.wait("scalar", tok)
        s.a(lambda e: e.activation(out=T["rowB"][:], in_=T["rowA"][:], func=AF.Sigmoid))
        s.a(lambda e: e.mul(out=T["rowB"][:], in_=T["rowB"][:], mul=-PI))
        s.a(lambda e: e.mul(out=T["rowC"][:], in_=T["rowB"][:], mul=0.5))
        s.a(lambda e: e.activation(out=T["rowC"][:], in_=T["rowC"][:], func=AF.Sin))
        s.a(lambda e: e.activation(out=T["rowB"][:], in_=T["rowB"][:], func=AF.Sin))
        s.a(lambda e: e.activation(out=T["rowC"][:], in_=T["rowC"][:], func=AF.Square))
        s.a(lambda e: e.mul(out=T["rowC"][:], in_=T["rowC"][:], mul=-2.0))
        s.a(lambda e: e.add(out=T["rowC"][:], in_=T["rowC"][:], add=1.0))
        ac = s.sig("scalar")
        s.wait("vector", ac)
        s.v(lambda e: e.tensor_mul(T["rowD"][:], T["rowB"][:], T["rowC"][:]))
        s.v(lambda e: e.tensor_mul(T["rowC"][:], T["rowB"][:], T["rowB"][:]))
        s.vd()
        s.v(lambda e: e.tensor_scalar(out=T["rowD"][:], in0=T["rowD"][:], scalar1=2.0, scalar2=None, op0=ALU.mult))
        s.v(lambda e: e.tensor_scalar(out=T["rowC"][:], in0=T["rowC"][:], scalar1=-2.0, scalar2=None, op0=ALU.mult))
        s.vd()
        s.v(lambda e: e.tensor_scalar(out=T["rowC"][:], in0=T["rowC"][:], scalar1=1.0, scalar2=None, op0=ALU.add))
        s.vd()
        ve = s.sig("vector")
        s.wait("tensor", ve)
        s.t(lambda e: e.matmul(s.P1[:, 0:16], T["ones_row"][:], T["rowD"][:], start=True, stop=True))
        s.t(lambda e: e.matmul(s.P1[:, 16:32], T["ones_row"][:], T["rowC"][:], start=True, stop=True))
        te = s.sig("tensor")
        s.wait("scalar", te)
        s.a(lambda e: e.copy(out=T["sind128"][:], in_=s.P1[:, 0:16]))
        s.a(lambda e: e.copy(out=T["cosd128"][:], in_=s.P1[:, 16:32]))
        ac2 = s.sig("scalar")
        s.wait("sync", ac2)
        tw1 = s.dma("sync", s.rap("dtr_d", 0, [[1, 1], [1, 16]]), T["rowD"][:])
        tw2 = s.dma("sync", s.rap("dtr_d", 16, [[1, 1], [1, 16]]), T["rowC"][:])
        s.wait("sync", tw1)
        s.wait("sync", tw2)
        tr1 = s.dma("sync", T["sindP"][:], s.rap("dtr_d", 0, [[1, 16], [1, 1]]))
        tr2 = s.dma("sync", T["cosdP"][:], s.rap("dtr_d", 16, [[1, 16], [1, 1]]))
        tc = s.dma("sync", T["slc"][:], C["sl_cos"])
        ts = s.dma("sync", T["sls"][:], C["sl_sin"])
        for tk in (tr1, tr2, tc, ts):
            s.wait("vector", tk)
        s.v(lambda e: e.tensor_scalar(out=T["rr1"][:], in0=T["slc"][0:16, :], scalar1=T["cosdP"][:], scalar2=None, op0=ALU.mult))
        s.v(lambda e: e.tensor_scalar(out=T["rr2"][:], in0=T["sls"][0:16, :], scalar1=T["sindP"][:], scalar2=None, op0=ALU.mult))
        s.v(lambda e: e.tensor_scalar(out=T["rr3"][:], in0=T["slc"][0:16, :], scalar1=T["sindP"][:], scalar2=None, op0=ALU.mult))
        s.v(lambda e: e.tensor_scalar(out=T["rr4"][:], in0=T["sls"][0:16, :], scalar1=T["cosdP"][:], scalar2=None, op0=ALU.mult))
        s.vd()
        s.v(lambda e: e.tensor_sub(T["rr1"][:], T["rr1"][:], T["rr2"][:]))
        s.v(lambda e: e.tensor_add(T["rr3"][:], T["rr3"][:], T["rr4"][:]))
        s.vd()
        s.v(lambda e: e.tensor_scalar(out=T["rr2"][:], in0=T["rr3"][:], scalar1=-1.0, scalar2=None, op0=ALU.mult))
        s.vd()
        ve4 = s.sig("vector")
        s.wait("sync", ve4)
        tw = []
        tw.append(s.dma("sync", s.rap("rot_d", 0, [[32, 16], [1, 32]]), T["rr1"][:]))
        tw.append(s.dma("sync", s.rap("rot_d", 512, [[32, 16], [1, 32]]), T["rr3"][:]))
        tw.append(s.dma("sync", s.rap("rot_d", 1024, [[32, 16], [1, 32]]), T["rr2"][:]))
        tw.append(s.dma("sync", s.rap("rot_d", 1536, [[32, 16], [1, 32]]), T["rr1"][:]))
        for tk in tw:
            s.wait("sync", tk)
        rtoks = []
        for g in range(4):
            rtoks.append(s.dma("sync", T["rT0"][32 * g:32 * g + 32, :], s.rap("rot_d", 0, [[32, 32], [1, 32]])))
            rtoks.append(s.dma("sync", T["rT1"][32 * g:32 * g + 32, :], s.rap("rot_d", 1024, [[32, 32], [1, 32]])))
        for tk in rtoks:
            s.wait("vector", tk)
        groups = [
            ("wseq", 352, 8, "wseq_f"),
            ("wkv", 1024, 2, "wkv_f"),
        ]
        for (nm, width, nch, fsrc) in groups:
            toks = [s.dma("gpsimd", T[nm + "_h"][:, 0:nch * width], C[fsrc]),
                    s.dma("sync", s.TL["AR0"][:, 0:nch * width], C[fsrc])]
            for tk in toks:
                s.wait("vector", tk)
            s.v(lambda e, nm=nm, width=width, nch=nch:
                e.tensor_sub(T[nm + "_l"][:, 0:nch * width], s.TL["AR0"][:, 0:nch * width], T[nm + "_h"][:, 0:nch * width]))
            s.vd()
            ve2 = s.sig("vector")
            s.wait("sync", ve2)
            s.wait("gpsimd", ve2)
        s.v(lambda e: e.memset(T["gidx16"][:], 0))
        s.v(lambda e: e.memset(T["fidx"][:], 0))
        s.vd()
        ctoks = []
        for (tile, src) in [("gb128", "gate_b128"), ("eio", "eio128"), ("nlf", "nlfix128"),
                            ("ident128s", "ident128"), ("eidx", "eidx_i16"), ("m16s", "m16")]:
            ctoks.append(s.dma("sync", T[tile][:], C[src]))
        for tk in ctoks:
            for eng in ("tensor", "vector", "scalar", "gpsimd"):
                s.wait(eng, tk)
        ti = s.dma("sync", T["rowA"][:, 0:4], C["idx_w4"])
        s.wait("tensor", ti)
        s.t(lambda e: e.matmul(s.P1[:, 32:36], T["ones_row"][:], T["rowA"][:, 0:4], start=True, stop=True))
        te3 = s.sig("tensor")
        s.wait("scalar", te3)
        s.a(lambda e: e.copy(out=T["idxw"][:], in_=s.P1[:, 32:36]))
        ve3 = s.sig("vector")
        for rr in range(16):
            s.v(lambda e, rr=rr: e.tensor_copy(out=T["rT0r"][:, 32 * rr:32 * rr + 32], in_=T["rT0"][:]))
            s.v(lambda e, rr=rr: e.tensor_copy(out=T["rT1r"][:, 32 * rr:32 * rr + 32], in_=T["rT1"][:]))
        s.vd()
        def _mkregs(e):
            for v in (128, 256, 512):
                r = e.alloc_register("nidx%d" % v)
                e.reg_mov(r, v)
                s.regs[v] = r
        s.g(_mkregs)
        s.g(lambda e: e.load_library(LIB_MLP))
        s.barrier()

    def p1(s):
        T, CV, C = s.TL, s.CV, s.C
        AR1, AR2 = T["AR1"], T["AR2"]
        p_x, p_xn = AR1[:, 0:1024], AR1[:, 1024:2048]
        p_ckv = AR2[:, 1024:1376]
        p_kup = AR2[:, 0:1024]
        one1 = T["stat"][:, 7:8]
        s.v(lambda e: e.memset(one1, 1.0))
        s.vd()
        prev_trig = None
        for TT in range(NT):
            r0 = 128 * TT
            t_x = s.dma("sync", p_x, C["x_seq"][r0:r0 + 128, :])
            if prev_trig is not None:
                s.wait("sync", prev_trig)
            t_ca = s.dma("sync", T["t16a"][:], C["tk63_cos"][r0:r0 + 128, :])
            t_cb = s.dma("sync", T["t16b"][:], C["tk63_sin"][r0:r0 + 128, :])
            s.wait("scalar", t_x)
            s.a(lambda e: e.activation(out=s.P2[:, 0:1024], in_=p_x, func=AF.Square))
            ac = s.sig("scalar")
            s.wait("vector", t_x)
            s.wait("vector", ac)
            st = T["stat"]
            s.v(lambda e: e.tensor_reduce(out=st[:, 0:1], in_=p_x, op=ALU.add, axis=AX.X))
            s.v(lambda e: e.tensor_reduce(out=st[:, 1:2], in_=s.P2[:, 0:1024], op=ALU.add, axis=AX.X))
            s.vd()
            s.v(lambda e: e.tensor_scalar(out=st[:, 0:1], in0=st[:, 0:1], scalar1=1.0 / 1024, scalar2=None, op0=ALU.mult))
            s.v(lambda e: e.tensor_scalar(out=st[:, 1:2], in0=st[:, 1:2], scalar1=1.0 / 1024, scalar2=None, op0=ALU.mult))
            s.vd()
            s.v(lambda e: e.tensor_mul(st[:, 2:3], st[:, 0:1], st[:, 0:1]))
            s.vd()
            s.v(lambda e: e.tensor_sub(st[:, 2:3], st[:, 1:2], st[:, 2:3]))
            s.vd()
            ve = s.sig("vector")
            s.wait("scalar", ve)
            s.a(lambda e: e.activation(out=st[:, 3:4], in_=st[:, 2:3], func=AF.Sqrt, bias=T["epst"][:]))
            ac2r = s.sig("scalar")
            s.wait("vector", ac2r)
            s.v(lambda e: e.reciprocal(st[:, 3:4], st[:, 3:4]))
            s.vd()
            ve2r = s.sig("vector")
            s.wait("scalar", ve2r)
            ac = s.sig("scalar")
            s.wait("vector", ac)
            s.v(lambda e: e.tensor_scalar(out=p_xn, in0=p_x, scalar1=st[:, 0:1], scalar2=None, op0=ALU.subtract))
            s.vd()
            s.v(lambda e: e.tensor_scalar(out=p_xn, in0=p_xn, scalar1=st[:, 3:4], scalar2=None, op0=ALU.mult))
            s.vd()
            s.split(p_xn, T["xh"][:], T["xl"][:])
            ve = s.sig("vector")
            s.wait("sync", ve)
            ttrs = []
            for c in range(8):
                ttrs.append(s.dmaT("sync", T["xnT_h"][:, 128 * c:128 * c + 128], T["xh"][:, 128 * c:128 * c + 128]))
                ttrs.append(s.dmaT("sync", T["xnT_l"][:, 128 * c:128 * c + 128], T["xl"][:, 128 * c:128 * c + 128]))
            s.dma("sync", C["xn_d"][r0:r0 + 128, 0:1024], T["xh"][:])
            s.dma("sync", C["xn_d"][r0:r0 + 128, 1024:2048], T["xl"][:])
            for tk in ttrs:
                s.wait("tensor", tk)
            wh = T["wseq_h"][:].rearrange("p (c n) -> p c n", c=8)
            wl = T["wseq_l"][:].rearrange("p (c n) -> p c n", c=8)
            for fc in range(8):
                s.mm3(s.P1[:, 0:352], T["xnT_h"][:, 128 * fc:128 * fc + 128],
                      T["xnT_l"][:, 128 * fc:128 * fc + 128], wh[:, fc, :], wl[:, fc, :],
                      start=(fc == 0), stop=(fc == 7))
            te = s.sig("tensor")
            s.wait("scalar", te)
            s.a(lambda e: e.copy(out=p_ckv, in_=s.P1[:, 0:352]))
            s.a(lambda e: e.activation(out=T["m32a"][:], in_=p_ckv[:, 320:352], func=AF.Abs))
            s.a(lambda e: e.activation(out=T["m32a"][:], in_=T["m32a"][:], func=AF.Exp, scale=-1.0))
            s.a(lambda e: e.activation(out=T["m32a"][:], in_=T["m32a"][:], func=AF.Ln, bias=one1))
            s.a(lambda e: e.activation(out=T["m32b"][:], in_=p_ckv[:, 320:352], func=AF.Relu))
            ac = s.sig("scalar")
            s.wait("vector", ac)
            s.v(lambda e: e.tensor_add(T["m32a"][:], T["m32a"][:], T["m32b"][:]))
            s.vd()
            s.split(p_ckv[:, 0:256], T["ckvh"][:], T["ckvl"][:])
            ve = s.sig("vector")
            s.wait("sync", ve)
            tctrs = []
            for c in range(2):
                tctrs.append(s.dmaT("sync", T["ckvT_h"][:, 128 * c:128 * c + 128], T["ckvh"][:, 128 * c:128 * c + 128]))
                tctrs.append(s.dmaT("sync", T["ckvT_l"][:, 128 * c:128 * c + 128], T["ckvl"][:, 128 * c:128 * c + 128]))
            s.wait("tensor", ve)
            s.t(lambda e: e.transpose(s.P2[0:64, 1024:1152], p_ckv[:, 256:320], T["ident128s"][:]))
            tki = s.sig("tensor")
            s.wait("scalar", tki)
            s.a(lambda e: e.copy(out=T["sc"][0:64, 0:128], in_=s.P2[0:64, 1024:1152]))
            ac = s.sig("scalar")
            s.wait("vector", ac)
            s.v(lambda e, r0=r0: e.tensor_copy(out=T["kiT_h"][0:64, r0:r0 + 128], in_=T["sc"][0:64, 0:128]))
            s.vd()
            s.v(lambda e, r0=r0: e.tensor_sub(T["kiT_l"][0:64, r0:r0 + 128], T["sc"][0:64, 0:128], T["kiT_h"][0:64, r0:r0 + 128]))
            s.vd()
            s.wait("vector", t_ca)
            s.wait("vector", t_cb)
            V = s.v
            t16a, t16b, t16c, t16d = T["t16a"], T["t16b"], T["t16c"], T["t16d"]
            m32a, m32b, m32c, m32d = T["m32a"], T["m32b"], T["m32c"], T["m32d"]
            V(lambda e: e.tensor_mul(t16c[:], t16a[:], T["cosd128"][:]))
            V(lambda e: e.tensor_mul(t16d[:], t16b[:], T["sind128"][:]))
            s.vd()
            V(lambda e: e.tensor_sub(t16c[:], t16c[:], t16d[:]))
            V(lambda e: e.tensor_mul(t16d[:], t16a[:], T["sind128"][:]))
            V(lambda e: e.tensor_mul(m32d[:, 0:16], t16b[:], T["cosd128"][:]))
            s.vd()
            V(lambda e: e.tensor_add(t16d[:], t16d[:], m32d[:, 0:16]))
            s.vd()
            ve_t63 = s.sig("vector")
            V(lambda e: e.tensor_mul(m32b[:, 0:16], m32a[:, 0:16], t16c[:]))
            V(lambda e: e.tensor_mul(m32d[:, 16:32], m32a[:, 16:32], t16d[:]))
            V(lambda e: e.tensor_mul(m32b[:, 16:32], m32a[:, 0:16], t16d[:]))
            V(lambda e: e.tensor_mul(m32d[:, 0:16], m32a[:, 16:32], t16c[:]))
            s.vd()
            V(lambda e: e.tensor_sub(m32b[:, 0:16], m32b[:, 0:16], m32d[:, 16:32]))
            V(lambda e: e.tensor_add(m32b[:, 16:32], m32b[:, 16:32], m32d[:, 0:16]))
            s.vd()
            s.wait("sync", ve_t63)
            t_c0 = s.dma("sync", t16a[:], C["tk0_cos"][r0:r0 + 128, :])
            t_c1 = s.dma("sync", t16b[:], C["tk0_sin"][r0:r0 + 128, :])
            s.wait("vector", t_c0)
            s.wait("vector", t_c1)
            V(lambda e: e.tensor_mul(t16c[:], t16a[:], T["cosd128"][:]))
            V(lambda e: e.tensor_mul(t16d[:], t16b[:], T["sind128"][:]))
            s.vd()
            V(lambda e: e.tensor_sub(t16c[:], t16c[:], t16d[:]))
            V(lambda e: e.tensor_mul(t16d[:], t16a[:], T["sind128"][:]))
            V(lambda e: e.tensor_mul(m32d[:, 0:16], t16b[:], T["cosd128"][:]))
            s.vd()
            V(lambda e: e.tensor_add(t16d[:], t16d[:], m32d[:, 0:16]))
            s.vd()
            V(lambda e: e.tensor_mul(m32c[:, 0:16], m32a[:, 0:16], t16c[:]))
            V(lambda e: e.tensor_mul(m32d[:, 16:32], m32a[:, 16:32], t16d[:]))
            s.vd()
            V(lambda e: e.tensor_sub(m32c[:, 0:16], m32c[:, 0:16], m32d[:, 16:32]))
            V(lambda e: e.tensor_mul(m32d[:, 0:16], m32a[:, 16:32], t16c[:]))
            V(lambda e: e.tensor_mul(m32d[:, 16:32], m32a[:, 0:16], t16d[:]))
            s.vd()
            V(lambda e: e.tensor_add(m32c[:, 16:32], m32d[:, 16:32], m32d[:, 0:16]))
            s.vd()
            prev_trig = s.sig("vector")
            for tk in tctrs:
                s.wait("tensor", tk)
            wh2 = T["wkv_h"][:].rearrange("p (c n) -> p c n", c=2)
            wl2 = T["wkv_l"][:].rearrange("p (c n) -> p c n", c=2)
            for half in range(2):
                for cc in range(2):
                    s.mm3(s.P2[:, 512 * half:512 * half + 512],
                          T["ckvT_h"][:, 128 * cc:128 * cc + 128], T["ckvT_l"][:, 128 * cc:128 * cc + 128],
                          wh2[:, cc, 512 * half:512 * half + 512], wl2[:, cc, 512 * half:512 * half + 512],
                          start=(cc == 0), stop=(cc == 1))
            te = s.sig("tensor")
            s.wait("scalar", te)
            s.a(lambda e: e.copy(out=p_kup, in_=s.P2[:, 0:1024]))
            ac = s.sig("scalar")
            s.wait("vector", ac)
            s.split(p_kup[:, 0:512], T["kvrow"][:, 0:512], T["kvrow"][:, 512:1024])
            s.split(p_kup[:, 512:1024], T["vrow"][:, 0:512], T["vrow"][:, 512:1024])
            for (src, base) in ((m32b, 1024), (m32c, 1280)):
                s.v(lambda e, src=src, base=base: e.tensor_copy(out=T["kvrow"][:, base:base + 32], in_=src[:]))
                s.vd()
                s.v(lambda e, src=src, base=base: e.tensor_sub(T["kvrow"][:, base + 128:base + 160], src[:], T["kvrow"][:, base:base + 32]))
                s.vd()
                for rcp in range(1, 4):
                    s.v(lambda e, base=base, rcp=rcp: e.tensor_copy(out=T["kvrow"][:, base + 32 * rcp:base + 32 * rcp + 32], in_=T["kvrow"][:, base:base + 32]))
                    s.v(lambda e, base=base, rcp=rcp: e.tensor_copy(out=T["kvrow"][:, base + 128 + 32 * rcp:base + 160 + 32 * rcp], in_=T["kvrow"][:, base + 128:base + 160]))
                s.vd()
            s.v(lambda e: e.tensor_copy(out=T["murow"][:, 0:16], in_=m32a[:, 0:16]))
            s.v(lambda e: e.tensor_copy(out=T["murow"][:, 256:272], in_=m32a[:, 16:32]))
            s.vd()
            s.v(lambda e: e.tensor_sub(T["murow"][:, 128:144], m32a[:, 0:16], T["murow"][:, 0:16]))
            s.v(lambda e: e.tensor_sub(T["murow"][:, 384:400], m32a[:, 16:32], T["murow"][:, 256:272]))
            s.vd()
            for blk in range(4):
                for rcp in range(1, 8):
                    s.v(lambda e, blk=blk, rcp=rcp: e.tensor_copy(
                        out=T["murow"][:, 128 * blk + 16 * rcp:128 * blk + 16 * rcp + 16],
                        in_=T["murow"][:, 128 * blk:128 * blk + 16]))
            s.vd()
            ve = s.sig("vector")
            s.wait("sync", ve)
            s.dma("sync", C["kv_d"][r0:r0 + 128, :], T["kvrow"][:])
            s.dma("sync", C["v_d"][r0:r0 + 128, :], T["vrow"][:])
            s.dma("sync", C["mu_d"][r0:r0 + 128, :], T["murow"][:])
            s.barrier()

    def _ln(s, src, dst, psl):
        T = s.TL
        st = T["stat"]
        s.a(lambda e: e.activation(out=psl, in_=src, func=AF.Square))
        ac = s.sig("scalar")
        s.wait("vector", ac)
        s.v(lambda e: e.tensor_reduce(out=st[:, 0:1], in_=src, op=ALU.add, axis=AX.X))
        s.v(lambda e: e.tensor_reduce(out=st[:, 1:2], in_=psl, op=ALU.add, axis=AX.X))
        s.vd()
        s.v(lambda e: e.tensor_scalar(out=st[:, 0:1], in0=st[:, 0:1], scalar1=1.0 / 1024, scalar2=None, op0=ALU.mult))
        s.v(lambda e: e.tensor_scalar(out=st[:, 1:2], in0=st[:, 1:2], scalar1=1.0 / 1024, scalar2=None, op0=ALU.mult))
        s.vd()
        s.v(lambda e: e.tensor_mul(st[:, 2:3], st[:, 0:1], st[:, 0:1]))
        s.vd()
        s.v(lambda e: e.tensor_sub(st[:, 2:3], st[:, 1:2], st[:, 2:3]))
        s.vd()
        ve = s.sig("vector")
        s.wait("scalar", ve)
        s.a(lambda e: e.activation(out=st[:, 3:4], in_=st[:, 2:3], func=AF.Sqrt, bias=T["epst"][:]))
        ac = s.sig("scalar")
        s.wait("vector", ac)
        s.v(lambda e: e.reciprocal(st[:, 3:4], st[:, 3:4]))
        s.vd()
        s.v(lambda e: e.tensor_scalar(out=dst, in0=src, scalar1=st[:, 0:1], scalar2=None, op0=ALU.subtract))
        s.vd()
        s.v(lambda e: e.tensor_scalar(out=dst, in0=dst, scalar1=st[:, 3:4], scalar2=None, op0=ALU.mult))
        s.vd()

    def _stream_w(s, hi_dst, lo_dst, src2d, nch, width):
        T = s.TL
        toks = [s.dma("gpsimd", hi_dst[:, 0:nch * width], src2d),
                s.dma("sync", T["AR0"][:, 0:nch * width], src2d)]
        for tk in toks:
            s.wait("vector", tk)
        s.v(lambda e: e.tensor_sub(lo_dst[:, 0:nch * width], T["AR0"][:, 0:nch * width], hi_dst[:, 0:nch * width]))
        s.vd()
        return s.sig("vector")

    def p2(s):
        n = NS
        for sl in range(n):
            s._slot(sl)

    def _slot(s, sl):
        T, CV, C = s.TL, s.CV, s.C
        srow = 128 * sl
        AR2 = T["AR2"]
        p_xn = AR2[:, 0:1024]
        p_cq = AR2[:, 1024:1800]
        p_qraw = AR2[:, 1800:2568]
        one1 = T["stat"][:, 7:8]
        t_x = s.dma("sync", T["xo"][:], C["x_own"][srow:srow + 128, :])
        s.wait("scalar", t_x)
        s.wait("vector", t_x)
        s._ln(T["xo"][:], p_xn, s.P2[:, 0:1024])
        s.split(p_xn, T["xh"][:], T["xl"][:])
        ve = s.sig("vector")
        s.wait("sync", ve)
        ttrs = []
        for c in range(8):
            ttrs.append(s.dmaT("sync", T["xnT_h"][:, 128 * c:128 * c + 128], T["xh"][:, 128 * c:128 * c + 128]))
            ttrs.append(s.dmaT("sync", T["xnT_l"][:, 128 * c:128 * c + 128], T["xl"][:, 128 * c:128 * c + 128]))
        s.dma("sync", C["xn2_d"][srow:srow + 128, 0:1024], T["xh"][:])
        s.dma("sync", C["xn2_d"][srow:srow + 128, 1024:2048], T["xl"][:])
        vw = s._stream_w(CV["wown_h"], CV["wown_l"], C["wown_f"], 8, 776)
        for tk in ttrs:
            s.wait("tensor", tk)
        s.wait("tensor", vw)
        wh = CV["wown_h"].rearrange("p (c n) -> p c n", c=8)
        wl = CV["wown_l"].rearrange("p (c n) -> p c n", c=8)
        for fc in range(8):
            s.mm3(s.P1[:, 0:512], T["xnT_h"][:, 128 * fc:128 * fc + 128], T["xnT_l"][:, 128 * fc:128 * fc + 128],
                  wh[:, fc, 0:512], wl[:, fc, 0:512], start=(fc == 0), stop=(fc == 7))
        for fc in range(8):
            s.mm3(s.P1[:, 512:776], T["xnT_h"][:, 128 * fc:128 * fc + 128], T["xnT_l"][:, 128 * fc:128 * fc + 128],
                  wh[:, fc, 512:776], wl[:, fc, 512:776], start=(fc == 0), stop=(fc == 7))
        te = s.sig("tensor")
        s.wait("scalar", te)
        s.a(lambda e: e.copy(out=p_cq, in_=s.P1[:, 0:776]))
        ac = s.sig("scalar")
        s.wait("vector", ac)
        st = T["stat"]
        s.v(lambda e: e.tensor_reduce(out=st[:, 4:5], in_=p_cq[:, 768:776], op=ALU.max, axis=AX.X))
        s.vd()
        s.v(lambda e: e.tensor_scalar(out=st[:, 4:5], in0=st[:, 4:5], scalar1=-1.0, scalar2=None, op0=ALU.mult))
        s.vd()
        ve = s.sig("vector")
        s.wait("scalar", ve)
        s.a(lambda e: e.activation(out=T["sfm"][:], in_=p_cq[:, 768:776], func=AF.Exp, bias=st[:, 4:5]))
        ac = s.sig("scalar")
        s.wait("vector", ac)
        s.v(lambda e: e.tensor_reduce(out=st[:, 5:6], in_=T["sfm"][:], op=ALU.add, axis=AX.X))
        s.vd()
        s.v(lambda e: e.reciprocal(st[:, 6:7], st[:, 5:6]))
        s.vd()
        s.v(lambda e: e.tensor_scalar(out=T["sfm"][:], in0=T["sfm"][:], scalar1=st[:, 6:7], scalar2=None, op0=ALU.mult))
        s.vd()
        s.v(lambda e: e.tensor_add(T["sfs"][:], T["sfm"][:], T["gb128"][:]))
        s.vd()
        s.v(lambda e: e.max(out=T["sfr"][:], in_=T["sfs"][:]))
        s.vd()
        s.v(lambda e: e.max_index(out=T["gidx"][:], in_max=T["sfr"][:], in_values=T["sfs"][:]))
        s.vd()
        s.v(lambda e: e.tensor_copy(out=T["g4"][:, 0:2], in_=T["gidx"][:, 0:2]))
        s.vd()
        for slot2 in range(2):
            s.v(lambda e, slot2=slot2: e.tensor_scalar(out=T["sfs"][:], in0=T["eio"][:], scalar1=T["g4"][:, slot2:slot2 + 1], scalar2=None, op0=ALU.is_equal))
            s.vd()
            s.v(lambda e: e.tensor_mul(T["sfs"][:], T["sfs"][:], T["sfm"][:]))
            s.vd()
            s.v(lambda e, slot2=slot2: e.tensor_reduce(out=T["g4"][:, 2 + slot2:3 + slot2], in_=T["sfs"][:], op=ALU.add, axis=AX.X))
            s.vd()
        ve = s.sig("vector")
        s.wait("sync", ve)
        s.dma("sync", C["gate_d"][srow:srow + 128, :], T["g4"][:])
        s.split(p_cq[:, 0:512], T["cqh"][:], T["cql"][:])
        ve = s.sig("vector")
        s.wait("sync", ve)
        tqs = []
        for c in range(4):
            tqs.append(s.dmaT("sync", T["cqT_h"][:, 128 * c:128 * c + 128], T["cqh"][:, 128 * c:128 * c + 128]))
            tqs.append(s.dmaT("sync", T["cqT_l"][:, 128 * c:128 * c + 128], T["cql"][:, 128 * c:128 * c + 128]))
        vw2 = s._stream_w(CV["wq2_h"], CV["wq2_l"], C["wq2_f"], 4, 768)
        for tk in tqs:
            s.wait("tensor", tk)
        s.wait("tensor", vw2)
        qh = CV["wq2_h"].rearrange("p (c n) -> p c n", c=4)
        ql = CV["wq2_l"].rearrange("p (c n) -> p c n", c=4)
        for cc in range(4):
            s.mm3(s.P2[:, 1024:1536], T["cqT_h"][:, 128 * cc:128 * cc + 128], T["cqT_l"][:, 128 * cc:128 * cc + 128],
                  qh[:, cc, 0:512], ql[:, cc, 0:512], start=(cc == 0), stop=(cc == 3))
        for cc in range(4):
            s.mm3(s.P2[:, 1536:1792], T["cqT_h"][:, 128 * cc:128 * cc + 128], T["cqT_l"][:, 128 * cc:128 * cc + 128],
                  qh[:, cc, 512:768], ql[:, cc, 512:768], start=(cc == 0), stop=(cc == 3))
        for hh in range(4):
            s.t(lambda e, hh=hh: e.transpose(s.P1[0:64, 1024 + 128 * hh:1152 + 128 * hh],
                                             p_cq[:, 512 + 64 * hh:576 + 64 * hh], T["ident128s"][:]))
        te = s.sig("tensor")
        s.wait("scalar", te)
        s.a(lambda e: e.copy(out=p_qraw, in_=s.P2[:, 1024:1792]))
        s.a(lambda e: e.copy(out=T["sc"][0:64, 0:512], in_=s.P1[0:64, 1024:1536]))
        s.a(lambda e: e.activation(out=T["qsp"][:], in_=p_qraw[:, 512:768], func=AF.Abs))
        s.a(lambda e: e.activation(out=T["qsp"][:], in_=T["qsp"][:], func=AF.Exp, scale=-1.0))
        s.a(lambda e: e.activation(out=T["qsp"][:], in_=T["qsp"][:], func=AF.Ln, bias=one1))
        s.a(lambda e: e.activation(out=CV["rtmp"][:, 0:256], in_=p_qraw[:, 512:768], func=AF.Relu))
        ac = s.sig("scalar")
        s.wait("vector", ac)
        s.v(lambda e: e.tensor_add(T["qsp"][:], T["qsp"][:], CV["rtmp"][:, 0:256]))
        s.v(lambda e: e.tensor_copy(out=T["qiT_h"][:], in_=T["sc"][0:64, 0:512]))
        s.vd()
        s.v(lambda e: e.tensor_sub(T["qiT_l"][:], T["sc"][0:64, 0:512], T["qiT_h"][:]))
        s.vd()
        t_ca = s.dma("sync", T["t16a"][:], C["tq_cos"][srow:srow + 128, :])
        t_cb = s.dma("sync", T["t16b"][:], C["tq_sin"][srow:srow + 128, :])
        s.wait("vector", t_ca)
        s.wait("vector", t_cb)
        V = s.v
        t16a, t16b, t16c, t16d = T["t16a"], T["t16b"], T["t16c"], T["t16d"]
        m32d = T["m32d"]
        V(lambda e: e.tensor_mul(t16c[:], t16a[:], T["cosd128"][:]))
        V(lambda e: e.tensor_mul(t16d[:], t16b[:], T["sind128"][:]))
        s.vd()
        V(lambda e: e.tensor_sub(t16c[:], t16c[:], t16d[:]))
        V(lambda e: e.tensor_mul(t16d[:], t16a[:], T["sind128"][:]))
        V(lambda e: e.tensor_mul(m32d[:, 0:16], t16b[:], T["cosd128"][:]))
        s.vd()
        V(lambda e: e.tensor_add(t16d[:], t16d[:], m32d[:, 0:16]))
        s.vd()
        qmat = AR2[:, 2568:3592]
        for hh in range(NH):
            bc = 64 * hh
            br = 512 + 32 * hh
            be = 768 + 32 * hh
            nu1 = T["qsp"][:, 32 * hh:32 * hh + 16]
            nu2 = T["qsp"][:, 32 * hh + 16:32 * hh + 32]
            V(lambda e, hh=hh, bc=bc: e.tensor_copy(out=qmat[:, bc:bc + 64], in_=p_qraw[:, 64 * hh:64 * hh + 64]))
            V(lambda e, br=br, nu1=nu1: e.tensor_mul(qmat[:, br:br + 16], nu1, t16c[:]))
            V(lambda e, nu2=nu2: e.tensor_mul(m32d[:, 0:16], nu2, t16d[:]))
            V(lambda e, br=br, nu1=nu1: e.tensor_mul(qmat[:, br + 16:br + 32], nu1, t16d[:]))
            V(lambda e, nu2=nu2: e.tensor_mul(m32d[:, 16:32], nu2, t16c[:]))
            s.vd()
            V(lambda e, br=br: e.tensor_sub(qmat[:, br:br + 16], qmat[:, br:br + 16], m32d[:, 0:16]))
            V(lambda e, br=br: e.tensor_add(qmat[:, br + 16:br + 32], qmat[:, br + 16:br + 32], m32d[:, 16:32]))
            s.vd()
            V(lambda e, br=br, be=be: e.tensor_mul(qmat[:, be:be + 16], qmat[:, br:br + 16], t16a[:]))
            V(lambda e, br=br: e.tensor_mul(m32d[:, 0:16], qmat[:, br + 16:br + 32], t16b[:]))
            V(lambda e, br=br, be=be: e.tensor_mul(qmat[:, be + 16:be + 32], qmat[:, br:br + 16], t16b[:]))
            V(lambda e, br=br: e.tensor_mul(m32d[:, 16:32], qmat[:, br + 16:br + 32], t16a[:]))
            s.vd()
            V(lambda e, be=be: e.tensor_sub(qmat[:, be:be + 16], qmat[:, be:be + 16], m32d[:, 0:16]))
            V(lambda e, be=be: e.tensor_add(qmat[:, be + 16:be + 32], qmat[:, be + 16:be + 32], m32d[:, 16:32]))
            s.vd()
        s.split(qmat, T["qmh"][:], T["qml"][:])
        ve = s.sig("vector")
        s.wait("sync", ve)
        for c in range(8):
            s.dmaT("sync", T["qT_h"][:, 128 * c:128 * c + 128], T["qmh"][:, 128 * c:128 * c + 128])
            s.dmaT("sync", T["qT_l"][:, 128 * c:128 * c + 128], T["qml"][:, 128 * c:128 * c + 128])
        s.barrier()
        tS = s.dma("sync", CV["S"], C["f2pack"][srow:srow + 128, :])
        tem = s.dma("sync", T["em"][:], C["edge_mask"][srow:srow + 128, :])
        s.wait("tensor", tS)
        s.wait("vector", tS)
        evtoks = {}
        actoks = {}
        for hh in range(NIH):
            for kb in range(8):
                bk = (hh * 8 + kb) % 4
                if (hh * 8 + kb) >= 4:
                    s.wait("tensor", evtoks[bk])
                lh = T["qiT_h"][0:64, 128 * hh:128 * hh + 128]
                ll = T["qiT_l"][0:64, 128 * hh:128 * hh + 128]
                rh = T["kiT_h"][0:64, 512 * kb:512 * kb + 512]
                rl = T["kiT_l"][0:64, 512 * kb:512 * kb + 512]
                s.mm3(s.P1[:, 512 * bk:512 * bk + 512], lh, ll, rh, rl, start=True, stop=True)
                te = s.sig("tensor")
                s.wait("scalar", te)
                rt = CV["rtmp"] if (hh * 8 + kb) % 2 == 0 else CV["rtmp2"]
                if (hh * 8 + kb) >= 2:
                    s.wait("scalar", actoks[(hh * 8 + kb) % 2])
                s.a(lambda e, rt=rt, bk=bk: e.activation(out=rt, in_=s.P1[:, 512 * bk:512 * bk + 512], func=AF.Relu))
                ac = s.sig("scalar")
                s.wait("vector", ac)
                s.v(lambda e, rt=rt, kb=kb, hh=hh: e.scalar_tensor_tensor(
                    out=CV["S"][:, 512 * kb:512 * kb + 512], in0=rt, scalar=T["idxw"][:, hh:hh + 1],
                    in1=CV["S"][:, 512 * kb:512 * kb + 512], op0=ALU.mult, op1=ALU.add))
                s.vd()
                evtoks[bk] = s.sig("vector")
                actoks[(hh * 8 + kb) % 2] = evtoks[bk]
        for bk in range(4):
            s.wait("vector", evtoks[bk])
        for r in range(4):
            s.v(lambda e, r=r: e.max(out=T["selv"][:, 8 * r:8 * r + 8], in_=CV["S"]))
            s.vd()
            s.v(lambda e, r=r: e.max_index(out=T["seli"][:, 8 * r:8 * r + 8], in_max=T["selv"][:, 8 * r:8 * r + 8], in_values=CV["S"]))
            s.vd()
            s.v(lambda e, r=r: e.match_replace(out=CV["S"], in_to_replace=T["selv"][:, 8 * r:8 * r + 8], in_values=CV["S"], imm_value=NEG))
            s.vd()
        s.wait("vector", tem)
        s.v(lambda e: e.tensor_copy(out=T["emu"][:], in_=T["em"][:]))
        s.vd()
        s.v(lambda e: e.tensor_copy(out=T["self32"][:], in_=T["seli"][:]))
        s.vd()
        s.v(lambda e: e.copy_predicated(out=T["self32"][:], mask=T["emu"][:].to_broadcast([128, NL]), data=T["nlf"][:]))
        s.vd()
        s.v(lambda e: e.tensor_copy(out=T["seli16"][:], in_=T["self32"][:]))
        s.vd()
        sw = T["seli16s"][:]
        swo = s.AP(tensor=sw.tensor, offset=sw.offset, ap=[list(sw.ap[0]), [2, 16], [1, 2]])
        sli = T["seli16"][:]
        swi = s.AP(tensor=sli.tensor, offset=sli.offset, ap=[list(sli.ap[0]), [1, 16], [16, 2]])
        s.v(lambda e, swo=swo, swi=swi: e.tensor_copy(out=swo, in_=swi))
        s.vd()
        ve = s.sig("vector")
        s.wait("sync", ve)
        swv = swo
        tw = s.dma("sync", s.rap("aidx_d", 0, [[2, 128], [256, 16], [1, 2]]), swv)
        tw2 = []
        for Qg in range(4):
            base = sw[32 * Qg:32 * Qg + 32, :]
            bv = s.AP(tensor=base.tensor, offset=base.offset, ap=[list(base.ap[0]), [2, 16], [1, 2]])
            tw2.append(s.dma("sync", s.rap("aidx2_d", 2 * Qg, [[8, 32], [256, 16], [1, 2]]), bv))
        s.wait("sync", tw)
        for tk in tw2:
            s.wait("sync", tk)
        tg1 = s.dma("sync", T["gidx16"][0:16, 0:256], s.rap("aidx_d", 0, [[256, 16], [1, 256]]))
        tg2 = s.dma("sync", T["gidx16"][16:32, 0:256], s.rap("aidx_d", 0, [[256, 16], [1, 256]]))
        tg3 = s.dma("sync", T["gidx16"][0:16, 256:512], s.rap("aidx2_d", 0, [[256, 16], [1, 256]]))
        tg4 = s.dma("sync", T["gidx16"][16:32, 256:512], s.rap("aidx2_d", 0, [[256, 16], [1, 256]]))
        tb1 = s.dma("sync", T["bidx"][:], C["band_idx"][srow:srow + 128, :])
        tb2 = s.dma("sync", T["bidx2"][:], C["band_idx2"][srow:srow + 128, :])
        for tk in (tg1, tg2, tg3, tg4, tb1, tb2):
            s.wait("gpsimd", tk)
        s.barrier()
        tbandg = s.dma("gpsimd", None, None) if False else None
        tband = s.gdma(lambda e, _h: e.dma_gather(out_ap=CV["bg"].rearrange("p (f n) -> p f n", f=12), in_ap=C["kv_d"],
                                   idxs_ap=T["bidx"][:], num_idxs=256, num_idxs_reg=s.regs[256],
                                   elem_size=KVROW, transpose=True).then_inc(_h, 16))
        s.wait("tensor", tband)
        s.wait("vector", tband)
        sc_ev = {}
        last_te = None
        last_vr = None
        for wave in range(2):
            for kb in range(8):
                s.wait("gpsimd", last_te)
                s.wait("gpsimd", last_vr)
                tkg = s.gdma(lambda e, _h, kb=kb: e.dma_gather(out_ap=CV["kg"].rearrange("p (f n) -> p f n", f=12), in_ap=C["kv_d"],
                                                  idxs_ap=T["gidx16"][:, 32 * kb:32 * kb + 32], num_idxs=512, num_idxs_reg=s.regs[512],
                                                  elem_size=KVROW, transpose=True).then_inc(_h, 16))
                tmg = s.gdma(lambda e, _h, kb=kb: e.dma_gather(out_ap=CV["mg"].rearrange("p (f n) -> p f n", f=4), in_ap=C["mu_d"],
                                                  idxs_ap=T["gidx16"][:, 32 * kb:32 * kb + 32], num_idxs=512, num_idxs_reg=s.regs[512],
                                                  elem_size=MUROW, transpose=True).then_inc(_h, 16))
                s.wait("vector", tmg)
                s.wait("vector", last_te)
                mgv = CV["mg"].rearrange("p (f n) -> p f n", f=4)
                s.v(lambda e, mgv=mgv: e.tensor_add(T["mrot"][:], mgv[:, 0, :], mgv[:, 1, :]))
                s.v(lambda e, mgv=mgv: e.tensor_add(CV["rtmp"][:], mgv[:, 2, :], mgv[:, 3, :]))
                s.vd()
                s.v(lambda e: e.tensor_mul(T["mrot"][:], T["mrot"][:], T["rT0r"][:]))
                s.v(lambda e: e.tensor_mul(CV["rtmp"][:], CV["rtmp"][:], T["rT1r"][:]))
                s.vd()
                s.v(lambda e: e.tensor_add(T["mrot"][:], T["mrot"][:], CV["rtmp"][:]))
                s.vd()
                s.v(lambda e: e.tensor_copy(out=CV["mhh"], in_=T["mrot"][:]))
                s.vd()
                s.v(lambda e: e.tensor_sub(CV["mhl"], T["mrot"][:], CV["mhh"]))
                s.vd()
                vrot = s.sig("vector")
                last_vr = vrot
                s.wait("tensor", vrot)
                s.wait("tensor", tkg)
                kgv = CV["kg"].rearrange("p (f n) -> p f n", f=12)
                for hw in range(4):
                    h = 4 * wave + hw
                    if kb > 1:
                        s.wait("tensor", sc_ev[(kb - 2, h)])
                    elif wave == 1:
                        s.wait("tensor", sc_ev[(kb + 6, hw)])
                    pb = (s.P1 if kb % 2 == 0 else s.P2)[:, 512 * hw:512 * hw + 512]
                    pbase = 64 * (h % 2)
                    lh = T["qT_h"][pbase:pbase + 64, 128 * (h // 2):128 * (h // 2) + 128]
                    ll = T["qT_l"][pbase:pbase + 64, 128 * (h // 2):128 * (h // 2) + 128]
                    rh = kgv[pbase:pbase + 64, h // 2, :]
                    rl = kgv[pbase:pbase + 64, 4 + h // 2, :]
                    s.mm3(pb, lh, ll, rh, rl, start=True, stop=False)
                    rbase = 32 * (h % 4)
                    tp = (96, 0) if (h % 4) == 3 else None
                    lh2 = T["qT_h"][rbase:rbase + 32, 128 * (4 + h // 4):128 * (4 + h // 4) + 128]
                    ll2 = T["qT_l"][rbase:rbase + 32, 128 * (4 + h // 4):128 * (4 + h // 4) + 128]
                    rh2 = CV["mhh"][rbase:rbase + 32, :]
                    rl2 = CV["mhl"][rbase:rbase + 32, :]
                    s.mm3(pb, lh2, ll2, rh2, rl2, start=False, stop=True, tp=tp)
                    te = s.sig("tensor")
                    last_te = te
                    if kb % 2 == 1:
                        a = kb // 2
                        s.wait("vector", te)
                        pbA = s.P1[:, 512 * hw:512 * hw + 512]
                        pbB = s.P2[:, 512 * hw:512 * hw + 512]
                        s.v(lambda e, a=a, h=h, pbA=pbA: e.tensor_copy(
                            out=CV["s_all"][32 * a:32 * a + 32, 512 * h:512 * h + 512], in_=pbA[32 * a:32 * a + 32, :]))
                        s.vd()
                        s.v(lambda e, a=a, h=h, pbB=pbB: e.copy_predicated(
                            out=CV["s_all"][32 * a:32 * a + 32, 512 * h:512 * h + 512],
                            mask=T["m16s"][32 * a:32 * a + 32, :].to_broadcast([32, 512]),
                            data=pbB[32 * a:32 * a + 32, :]))
                        s.vd()
                        ev = s.sig("vector")
                        sc_ev[(kb - 1, h)] = ev
                        sc_ev[(kb, h)] = ev
                        prev_te = ev
        for kk in [(6, h) for h in range(NH)] + [(7, h) for h in range(NH)]:
            s.wait("tensor", sc_ev[kk])
        bgv = CV["bg"].rearrange("p (f n) -> p f n", f=12)
        bev = {}
        for h in range(NH):
            pb = s.P1[:, 512 * (h % 4):512 * (h % 4) + 512]
            if h >= 4:
                s.wait("tensor", bev[h - 4])
            pbase = 64 * (h % 2)
            s.mm3(pb[:, 0:256], T["qT_h"][pbase:pbase + 64, 128 * (h // 2):128 * (h // 2) + 128],
                  T["qT_l"][pbase:pbase + 64, 128 * (h // 2):128 * (h // 2) + 128],
                  bgv[pbase:pbase + 64, h // 2, :], bgv[pbase:pbase + 64, 4 + h // 2, :], start=True, stop=False)
            rbase = 32 * (h % 4)
            tp = (96, 0) if (h % 4) == 3 else None
            s.mm3(pb[:, 0:256], T["qT_h"][rbase:rbase + 32, 128 * (6 + h // 4):128 * (6 + h // 4) + 128],
                  T["qT_l"][rbase:rbase + 32, 128 * (6 + h // 4):128 * (6 + h // 4) + 128],
                  bgv[rbase:rbase + 32, 8, :], bgv[rbase:rbase + 32, 9, :], start=False, stop=True, tp=tp)
            te = s.sig("tensor")
            s.wait("scalar", te)
            s.a(lambda e, h=h, pb=pb: e.copy(out=CV["band_all"][:, 192 * h:192 * h + 192], in_=pb[:, 0:192]))
            bev[h % 4] = s.sig("scalar")
        for k in list(bev.values()):
            s.wait("tensor", k)
        eev = {}
        for h in range(NH):
            pb = s.P2[:, 512 * (h % 4):512 * (h % 4) + 512]
            if h >= 4:
                s.wait("tensor", eev[h - 4])
            pbase = 64 * (h % 2)
            s.mm3(pb[:, 0:96], T["qT_h"][pbase:pbase + 64, 128 * (h // 2):128 * (h // 2) + 128],
                  T["qT_l"][pbase:pbase + 64, 128 * (h // 2):128 * (h // 2) + 128],
                  bgv[pbase:pbase + 64, h // 2, 63:159], bgv[pbase:pbase + 64, 4 + h // 2, 63:159], start=True, stop=False)
            rbase = 32 * (h % 4)
            tp = (96, 0) if (h % 4) == 3 else None
            s.mm3(pb[:, 0:96], T["qT_h"][rbase:rbase + 32, 128 * (4 + h // 4):128 * (4 + h // 4) + 128],
                  T["qT_l"][rbase:rbase + 32, 128 * (4 + h // 4):128 * (4 + h // 4) + 128],
                  bgv[rbase:rbase + 32, 10, 63:159], bgv[rbase:rbase + 32, 11, 63:159], start=False, stop=True, tp=tp)
            te = s.sig("tensor")
            s.wait("scalar", te)
            s.a(lambda e, h=h, pb=pb: e.copy(out=CV["edge_all"][:, 96 * h:96 * h + 96], in_=pb[:, 0:96]))
            eev[h % 4] = s.sig("scalar")
        ace = s.sig("scalar")
        s.wait("sync", ace)
        s.barrier()
        t1 = s.dma("sync", s.rap("s_flat", 0, [[4096, 128], [1, 4096]]), CV["s_all"])
        t2 = s.dma("sync", s.rap("b_flat", 0, [[1536, 128], [1, 1536]]), CV["band_all"])
        t3 = s.dma("sync", s.rap("e_flat", 0, [[768, 128], [1, 768]]), CV["edge_all"])
        for tk in (t1, t2, t3):
            s.wait("sync", tk)
        scv = T["sc"][:].rearrange("p (h j) -> p h j", h=8)
        t4 = s.dma("sync", scv[:, :, 0:64], s.rap("b_flat", 0, [[1537, 128], [192, 8], [1, 64]]))
        t5s = []
        for aa in range(8):
            t5s.append(s.dma("sync", T["sc"][16 * aa:16 * aa + 16, :].rearrange("r (h j) -> r h j", h=8)[:, :, 64:96],
                       s.rap("s_flat", 65536 * aa, [[4128, 16], [512, 8], [1, 32]])))
        t6 = s.dma("sync", CV["sc2"], s.rap("e_flat", 0, [[768, 128], [1, 768]]))
        for tk in [t4, t6] + t5s:
            s.wait("vector", tk)
        s.v(lambda e: e.copy_predicated(out=T["sc"][0:64, :], mask=T["emu"][0:64, :].to_broadcast([64, 768]), data=CV["sc2"][0:64, :]))
        s.vd()
        s.v(lambda e: e.tensor_reduce(out=T["sfm"][:], in_=scv, op=ALU.max, axis=AX.X))
        s.vd()
        s.v(lambda e: e.tensor_scalar(out=T["sfmn"][:], in0=T["sfm"][:], scalar1=-SCALE, scalar2=None, op0=ALU.mult))
        s.vd()
        ve = s.sig("vector")
        s.wait("scalar", ve)
        for h in range(NH):
            s.a(lambda e, h=h: e.activation(out=T["sc"][:, 96 * h:96 * h + 96], in_=T["sc"][:, 96 * h:96 * h + 96],
                                            func=AF.Exp, bias=T["sfmn"][:, h:h + 1], scale=SCALE))
        ac = s.sig("scalar")
        s.wait("vector", ac)
        s.v(lambda e: e.tensor_reduce(out=T["sfs"][:], in_=scv, op=ALU.add, axis=AX.X))
        s.vd()
        s.v(lambda e: e.reciprocal(T["sfr"][:], T["sfs"][:]))
        s.vd()
        for h in range(NH):
            s.v(lambda e, h=h: e.tensor_scalar(out=T["sc"][:, 96 * h:96 * h + 96], in0=T["sc"][:, 96 * h:96 * h + 96],
                                               scalar1=T["sfr"][:, h:h + 1], scalar2=None, op0=ALU.mult))
        s.vd()
        s.split(T["sc"][:], CV["a_h"], CV["a_l"])
        s.v(lambda e: e.tensor_scalar(out=T["stat"][:, 4:5], in0=T["em"][:], scalar1=0.0, scalar2=None, op0=ALU.is_equal))
        s.vd()
        ve = s.sig("vector")
        s.wait("sync", ve)
        for (hl, asrc) in ((0, CV["a_h"]), (1, CV["a_l"])):
            av = asrc.rearrange("p (h j) -> p h j", h=8)
            s.v(lambda e, av=av: e.tensor_scalar(out=CV["abst"].rearrange("p (h j) -> p h j", h=8), in0=av[:, :, 0:64],
                                                 scalar1=T["stat"][:, 4:5], scalar2=None, op0=ALU.mult))
            s.vd()
            ve = s.sig("vector")
            s.wait("sync", ve)
            tb = s.dma("sync", s.rap("ab_flat", hl * (128 * 8 * 192) + 64, [[1536, 128], [192, 8], [1, 64]]),
                       CV["abst"].rearrange("p (h j) -> p h j", h=8))
            s.wait("vector", tb)
            s.v(lambda e, av=av: e.tensor_scalar(out=CV["abst"].rearrange("p (h j) -> p h j", h=8), in0=av[:, :, 0:64],
                                                 scalar1=T["em"][:], scalar2=None, op0=ALU.mult))
            s.vd()
            ve = s.sig("vector")
            s.wait("sync", ve)
            te_ = s.dma("sync", s.rap("ae_flat", hl * (128 * 8 * 64), [[512, 128], [64, 8], [1, 64]]),
                        CV["abst"].rearrange("p (h j) -> p h j", h=8))
            s.wait("vector", te_)
        abtoks = []
        for (hl, asrc) in ((0, CV["a_h"]), (1, CV["a_l"])):
            avn = asrc.rearrange("p (h j) -> p h j", h=8)[:, :, 64:96]
            swo = CV["abst"][:, 0:256]
            swv = s.AP(tensor=swo.tensor, offset=swo.offset, ap=[list(swo.ap[0]), [1, 8], [8, 32]])
            s.v(lambda e, swv=swv, avn=avn: e.tensor_copy(out=swv, in_=avn))
            s.vd()
            ve_sw = s.sig("vector")
            s.wait("sync", ve_sw)
            tw_ = s.dma("sync", s.rap("anl_d", hl * 32768, [[256, 128], [1, 256]]), swo)
            s.wait("sync", tw_)
            s.wait("vector", tw_)
            dstc = CV["ablk_h"] if hl == 0 else CV["ablk_l"]
            for Q in range(4):
                dv = dstc[32 * Q:32 * Q + 32, :]
                dvv = s.AP(tensor=dv.tensor, offset=dv.offset + 8 * Q, ap=[list(dv.ap[0]), [32, 32], [1, 8]])
                abtoks.append(s.dma("sync", dvv, s.rap("anl_d", hl * 32768 + 8192 * Q, [[8, 32], [256, 32], [1, 8]])))
        s.barrier()
        for tk in abtoks:
            s.wait("tensor", tk)
        tvb1 = s.gdma(lambda e, _h: e.dma_gather(out_ap=CV["vband"][:, 0:1024].rearrange("p (n e2) -> p n e2", n=1), in_ap=C["v_d"],
                                   idxs_ap=T["bidx"][:, 0:8], num_idxs=128, num_idxs_reg=s.regs[128],
                                   elem_size=VROW).then_inc(_h, 16))
        tvb2 = s.gdma(lambda e, _h: e.dma_gather(out_ap=CV["vband"][:, 1024:2048].rearrange("p (n e2) -> p n e2", n=1), in_ap=C["v_d"],
                                   idxs_ap=T["bidx2"][:, 0:8], num_idxs=128, num_idxs_reg=s.regs[128],
                                   elem_size=VROW).then_inc(_h, 16))
        tvb3 = s.gdma(lambda e, _h: e.dma_gather(out_ap=CV["vband"][:, 2048:3072].rearrange("p (n e2) -> p n e2", n=1), in_ap=C["v_d"],
                                   idxs_ap=T["eidx"][:], num_idxs=128, num_idxs_reg=s.regs[128],
                                   elem_size=VROW).then_inc(_h, 16))
        s.wait("tensor", tvb1)
        s.wait("tensor", tvb2)
        s.wait("tensor", tvb3)
        avev = {}
        for b in range(8):
            if b > 0:
                s.wait("gpsimd", avev[4 * b - 1])
            tvg = s.gdma(lambda e, _h, b=b: e.dma_gather(out_ap=CV["vg"].rearrange("p (f n) -> p f n", f=4), in_ap=C["v_d"],
                                            idxs_ap=T["gidx16"][:, 256 + 32 * b:256 + 32 * b + 32], num_idxs=512, num_idxs_reg=s.regs[512],
                                            elem_size=VROW).then_inc(_h, 16))
            s.wait("tensor", tvg)
            vgv = CV["vg"].rearrange("p (f n) -> p f n", f=4)
            bh_ = CV["ablk_h"]; bl_ = CV["ablk_l"]
            for mi in range(4):
                m = 4 * b + mi
                if m >= 2:
                    s.wait("tensor", avev[m - 2])
                lmh = s.AP(tensor=bh_.tensor, offset=bh_.offset + 32 * m, ap=[list(bh_.ap[0]), [1, 32]])
                lml = s.AP(tensor=bl_.tensor, offset=bl_.offset + 32 * m, ap=[list(bl_.ap[0]), [1, 32]])
                s.mm3(s.P1[0:32, 512 * (m % 2):512 * (m % 2) + 512], lmh, lml,
                      vgv[:, mi, 0:512], vgv[:, mi, 512:1024], start=True, stop=True)
                te = s.sig("tensor")
                s.wait("scalar", te)
                s.a(lambda e, m=m: e.copy(out=CV["sc2"][32 * (m % 4):32 * (m % 4) + 32, 0:512],
                                          in_=s.P1[0:32, 512 * (m % 2):512 * (m % 2) + 512]))
                avev[m] = s.sig("scalar")
            s.wait("sync", avev[4 * b + 3])
            tst = s.dma("sync", s.rap("av_flat", 0, [[512, 128], [1, 512]]), CV["sc2"][:, 0:512])
            s.wait("sync", tst)
            tors = []
            for Qp in range(4):
                tors.append(s.dma("sync", CV["o"][32 * Qp + 4 * b:32 * Qp + 4 * b + 4, :].rearrange("q (h d) -> q h d", h=8),
                            s.rap("av_flat", 4096 * Qp, [[16384, 4], [576, 8], [1, 64]])))
            for tor in tors:
                s.wait("scalar", tor)
                s.wait("vector", tor)
                s.wait("sync", tor)
        for kk in range(30, 32):
            s.wait("tensor", avev[kk])
        bavev = {}
        last_bte = None
        for g in range(2):
            for h in range(NH):
                idx2 = g * 8 + h
                s.wait("sync", last_bte)
                taT = s.dma("sync", T["abT"][0:64, :], s.rap("ab_flat", (64 * g) * 1536 + h * 192 + 64, [[1535, 64], [1, 128]]))
                tbT = s.dma("sync", T["abT"][64:128, :], s.rap("ab_flat", 128 * 8 * 192 + (64 * g) * 1536 + h * 192 + 64, [[1535, 64], [1, 128]]))
                s.wait("sync", taT)
                s.wait("sync", tbT)
                t1_ = s.dmaT("sync", T["aband_h"][:], T["abT"][0:64, :])
                t2_ = s.dmaT("sync", T["aband_l"][:], T["abT"][64:128, :])
                s.wait("sync", t1_)
                s.wait("sync", t2_)
                s.wait("tensor", t1_)
                s.wait("tensor", t2_)
                bk = idx2 % 4
                if idx2 >= 4:
                    s.wait("tensor", bavev[idx2 - 4])
                vb = CV["vband"][:, 1024 * g:1024 * g + 1024]
                s.mm3(s.P2[0:64, 512 * bk:512 * bk + 512], T["aband_h"][:], T["aband_l"][:],
                      vb[:, 0:512], vb[:, 512:1024], start=True, stop=True)
                te = s.sig("tensor")
                last_bte = te
                s.wait("vector", te)
                s.v(lambda e, g=g, h=h, bk=bk: e.tensor_add(CV["o"][64 * g:64 * g + 64, 64 * h:64 * h + 64],
                                                            CV["o"][64 * g:64 * g + 64, 64 * h:64 * h + 64],
                                                            s.P2[0:64, 512 * bk + 64 * h:512 * bk + 64 * h + 64]))
                s.vd()
                bavev[idx2] = s.sig("vector")
        for kk in range(12, 16):
            s.wait("tensor", bavev[kk])
        eavev = {}
        for h in range(NH):
            s.wait("sync", last_bte)
            taT = s.dma("sync", T["abT"][0:64, 0:64], s.rap("ae_flat", h * 64, [[512, 64], [1, 64]]))
            tbT = s.dma("sync", T["abT"][64:128, 0:64], s.rap("ae_flat", 128 * 8 * 64 + h * 64, [[512, 64], [1, 64]]))
            s.wait("sync", taT)
            s.wait("sync", tbT)
            t1_ = s.dmaT("sync", T["aedge_h"][:], T["abT"][0:64, :])
            t2_ = s.dmaT("sync", T["aedge_l"][:], T["abT"][64:128, :])
            s.wait("sync", t1_)
            s.wait("sync", t2_)
            s.wait("tensor", t1_)
            s.wait("tensor", t2_)
            bk = h % 4
            if h >= 4:
                s.wait("tensor", eavev[h - 4])
            vb = CV["vband"][:, 2048:3072]
            s.mm3(s.P2[0:64, 512 * bk:512 * bk + 512], T["aedge_h"][0:64, :], T["aedge_l"][0:64, :],
                  vb[0:64, 0:512], vb[0:64, 512:1024], start=True, stop=True)
            te = s.sig("tensor")
            last_bte = te
            s.wait("vector", te)
            s.v(lambda e, h=h, bk=bk: e.tensor_add(CV["o"][0:64, 64 * h:64 * h + 64],
                                                   CV["o"][0:64, 64 * h:64 * h + 64],
                                                   s.P2[0:64, 512 * bk + 64 * h:512 * bk + 64 * h + 64]))
            s.vd()
            eavev[h] = s.sig("vector")
        s.barrier()
        s.split(CV["o"], T["oh"][:], T["ol"][:])
        ve = s.sig("vector")
        s.wait("sync", ve)
        tos = []
        for c in range(4):
            tos.append(s.dmaT("sync", T["oT_h"][:, 128 * c:128 * c + 128], T["oh"][:, 128 * c:128 * c + 128]))
            tos.append(s.dmaT("sync", T["oT_l"][:, 128 * c:128 * c + 128], T["ol"][:, 128 * c:128 * c + 128]))
        vwo = s._stream_w(CV["wout_h"], CV["wout_l"], C["wout_f"], 4, 1024)
        for tk in tos:
            s.wait("tensor", tk)
        s.wait("tensor", vwo)
        oh_ = CV["wout_h"].rearrange("p (c n) -> p c n", c=4)
        ol_ = CV["wout_l"].rearrange("p (c n) -> p c n", c=4)
        for half in range(2):
            for cc in range(4):
                s.mm3(s.P1[:, 512 * half:512 * half + 512],
                      T["oT_h"][:, 128 * cc:128 * cc + 128], T["oT_l"][:, 128 * cc:128 * cc + 128],
                      oh_[:, cc, 512 * half:512 * half + 512], ol_[:, cc, 512 * half:512 * half + 512],
                      start=(cc == 0), stop=(cc == 3))
        te = s.sig("tensor")
        s.wait("vector", te)
        s.v(lambda e: e.tensor_add(CV["base"][:, 0:512], T["xo"][:, 0:512], s.P1[:, 0:512]))
        s.v(lambda e: e.tensor_add(CV["base"][:, 512:1024], T["xo"][:, 512:1024], s.P1[:, 512:1024]))
        s.vd()
        ve = s.sig("vector")
        s.wait("sync", ve)
        s.c_lastout = s.dma("sync", C["out"][srow:srow + 128, :], CV["base"][:])
        s.wait("sync", s.c_lastout)
        s.barrier()

    def p3(s):
        T, CV, C = s.TL, s.CV, s.C
        AR1, AR2 = T["AR1"], T["AR2"]
        toks = []
        for (qi, tile) in enumerate(("gA", "gB", "gC", "gD")):
            toks.append(s.dma("sync", T[tile][:], s.rap("gate_d", qi, [[4, 16], [64, 128]])))
        toks.append(s.dma("sync", T["csq"][:], C["cseqid"]))
        toks.append(s.dma("sync", T["slio"][:], C["slot_iota"]))
        toks.append(s.dma("sync", T["oidx"][:], C["ownid_i16"]))
        for tk in toks:
            s.wait("vector", tk)
            s.wait("gpsimd", tk)
        s.v(lambda e: e.memset(T["z16"][:], 0.0))
        s.vd()
        s.g(lambda e: e.load_library(LIB_SG))
        gpl = s.sig("gpsimd")
        prev_sts = []
        for ee in range(E):
            fe = float(ee)
            s.wait("vector", gpl if ee == 0 else None)
            for tk in prev_sts:
                s.wait("gpsimd", tk)
            s.v(lambda e, fe=fe: e.tensor_scalar(out=T["gE"][:], in0=T["gA"][:], scalar1=fe, scalar2=None, op0=ALU.is_equal))
            s.v(lambda e, fe=fe: e.tensor_scalar(out=T["gF"][:], in0=T["gB"][:], scalar1=fe, scalar2=None, op0=ALU.is_equal))
            s.vd()
            s.v(lambda e: e.tensor_mul(T["gG"][:], T["gE"][:], T["gC"][:]))
            s.v(lambda e: e.tensor_mul(T["gF"][:], T["gF"][:], T["gD"][:]))
            s.vd()
            s.v(lambda e: e.tensor_add(T["gG"][:], T["gG"][:], T["gF"][:]))
            s.v(lambda e, fe=fe: e.tensor_scalar(out=T["gF"][:], in0=T["gB"][:], scalar1=fe, scalar2=None, op0=ALU.is_equal))
            s.vd()
            s.v(lambda e: e.tensor_add(T["gE"][:], T["gE"][:], T["gF"][:]))
            s.vd()
            s.v(lambda e: e.tensor_scalar(out=T["gF"][:], in0=T["csq"][:], scalar1=1.0, scalar2=None, op0=ALU.add))
            s.v(lambda e: e.tensor_scalar(out=T["gG"][:], in0=T["gG"][:], scalar1=1.0, scalar2=None, op0=ALU.add))
            s.vd()
            s.v(lambda e: e.tensor_mul(T["gF"][:], T["gE"][:], T["gF"][:]))
            s.v(lambda e: e.tensor_mul(T["gG"][:], T["gE"][:], T["gG"][:]))
            s.vd()
            s.v(lambda e: e.tensor_scalar(out=T["gF"][:], in0=T["gF"][:], scalar1=-1.0, scalar2=None, op0=ALU.add))
            s.v(lambda e: e.tensor_scalar(out=T["gG"][:], in0=T["gG"][:], scalar1=-1.0, scalar2=None, op0=ALU.add))
            s.vd()
            ve = s.sig("vector")
            s.wait("gpsimd", ve)
            s.g(lambda e: e.memset(T["sga"][:], -1.0))
            s.g(lambda e: e.sparse_gather(out=T["sga"][:], in_=T["gF"][:], num_found=T["sgn"][:]))
            s.g(lambda e: e.sparse_gather(out=T["sgw"][:], in_=T["gG"][:], num_found=T["sgn"][:]))
            gp = s.sig("gpsimd")
            s.wait("vector", gp)
            s.v(lambda e: e.tensor_copy(out=T["nffrow"][:], in_=T["sgn"][:]))
            s.vd()
            ve = s.sig("vector")
            s.wait("tensor", ve)
            s.t(lambda e: e.matmul(s.P1[0:16, 36:37], T["ones_row"][:, 0:16], T["nffrow"][:], start=True, stop=True))
            te = s.sig("tensor")
            s.wait("scalar", te)
            s.a(lambda e: e.copy(out=T["nff"][:], in_=s.P1[0:16, 36:37]))
            ac = s.sig("scalar")
            s.wait("vector", ac)
            s.v(lambda e: e.tensor_scalar(out=T["vld"][:], in0=T["slio"][:], scalar1=T["nff"][:], scalar2=None, op0=ALU.is_lt))
            s.vd()
            s.v(lambda e: e.tensor_scalar(out=T["ivld"][:], in0=T["vld"][:], scalar1=0.0, scalar2=None, op0=ALU.is_equal))
            s.vd()
            s.v(lambda e: e.tensor_copy(out=T["ivldu"][:], in_=T["ivld"][:]))
            s.vd()
            s.v(lambda e: e.copy_predicated(out=T["sga"][:], mask=T["ivldu"][:], data=T["z16"][:]))
            s.v(lambda e: e.copy_predicated(out=T["sgw"][:], mask=T["ivldu"][:], data=T["z16"][:]))
            s.vd()
            s.v(lambda e: e.tensor_copy(out=T["sgi16"][:], in_=T["sga"][:]))
            s.vd()
            ve = s.sig("vector")
            s.wait("sync", ve)
            prev_sts = [
                s.dma("sync", s.rap("ci_d", ee * CAP, [[1, 16], [16, NCT * 8]]), T["sgi16"][:]),
                s.dma("sync", s.rap("cw_d", ee * CAP, [[1, 16], [16, NCT * 8]]), T["sgw"][:]),
            ]
        s.barrier()
        s.g(lambda e: e.load_library(LIB_MLP))
        gpl = s.sig("gpsimd")
        s.wait("sync", gpl)
        last_f_te = None
        last_w2_te = None
        jobs = [(e2, list(range(NCT))) for e2 in range(E)]
        jobs += [(E, list(range(0, 6))), (E, list(range(6, 12))), (E, list(range(12, 16)))]
        for (ee, ctlist) in jobs:
            shared = (ee == E)
            nct = len(ctlist)
            if not shared:
                ti1 = s.dma("sync", T["fidx"][0:16, :], s.rap("ci_d", ee * CAP, [[1, 16], [16, NCT * 8]]))
                ti2 = s.dma("sync", T["fidx"][16:32, :], s.rap("ci_d", ee * CAP, [[1, 16], [16, NCT * 8]]))
                s.wait("gpsimd", ti1)
                s.wait("gpsimd", ti2)
            ysb = CV["y"]
            for iq in range(4):
                wsrcs = []
                if shared:
                    w1s = C["sw1f"][:, 4096 * iq:4096 * iq + 4096]
                    w3s = C["sw3f"][:, 4096 * iq:4096 * iq + 4096]
                    w2s = C["sw2f"][:, 4096 * iq:4096 * iq + 4096]
                else:
                    w1s = C["w1f"][ee, :, 4096 * iq:4096 * iq + 4096]
                    w3s = C["w3f"][ee, :, 4096 * iq:4096 * iq + 4096]
                    w2s = C["w2f"][ee, :, 4096 * iq:4096 * iq + 4096]
                stage = AR1[:, 0:4096]
                s.wait("gpsimd", last_w2_te)
                s.wait("sync", last_w2_te)
                wv = None
                for (dsth, dstl, src, nc_, wid) in [
                        (CV["f_w1h"], CV["f_w1l"], w1s, 8, 512),
                        (CV["f_w3h"], CV["f_w3l"], w3s, 8, 512),
                        (CV["f_w2h"], CV["f_w2l"], w2s, 4, 1024)]:
                    th_ = s.dma("gpsimd", dsth[:, 0:nc_ * wid], src)
                    tf_ = s.dma("sync", stage[:, 0:nc_ * wid], src)
                    s.wait("vector", th_)
                    s.wait("vector", tf_)
                    s.v(lambda e, dsth=dsth, dstl=dstl, nc_=nc_, wid=wid: e.tensor_sub(dstl[:, 0:nc_ * wid], stage[:, 0:nc_ * wid], dsth[:, 0:nc_ * wid]))
                    s.vd()
                    wv = s.sig("vector")
                    s.wait("sync", wv)
                    s.wait("gpsimd", wv)
                s.wait("tensor", wv)
                for ci_, ct in enumerate(ctlist):
                    idxap = T["oidx"][:, 8 * ct:8 * ct + 8] if shared else T["fidx"][:, 8 * ct:8 * ct + 8]
                    xg = AR2[:, 0:1024].bitcast(BF16)
                    s.wait("gpsimd", last_f_te)
                    txg = s.gdma(lambda e, _h, idxap=idxap, xg=xg: e.dma_gather(
                        out_ap=xg.rearrange("p (f n) -> p f n", f=16), in_ap=C["xn2_d"], idxs_ap=idxap,
                        num_idxs=128, num_idxs_reg=s.regs[128], elem_size=XNROW, transpose=True).then_inc(_h, 16))
                    s.wait("tensor", txg)
                    xgv = xg.rearrange("p (f n) -> p f n", f=16)
                    w1h = CV["f_w1h"].rearrange("p (c n) -> p c n", c=8)
                    w1l = CV["f_w1l"].rearrange("p (c n) -> p c n", c=8)
                    w3h = CV["f_w3h"].rearrange("p (c n) -> p c n", c=8)
                    w3l = CV["f_w3l"].rearrange("p (c n) -> p c n", c=8)
                    for fc in range(8):
                        s.mm3(s.P2[:, 0:512], xgv[:, fc, :], xgv[:, 8 + fc, :],
                              w1h[:, fc, :], w1l[:, fc, :], start=(fc == 0), stop=(fc == 7))
                    for fc in range(8):
                        s.mm3(s.P2[:, 512:1024], xgv[:, fc, :], xgv[:, 8 + fc, :],
                              w3h[:, fc, :], w3l[:, fc, :], start=(fc == 0), stop=(fc == 7))
                    te = s.sig("tensor")
                    last_f_te = te
                    s.wait("scalar", te)
                    s.a(lambda e: e.activation(out=AR2[:, 2048:2560], in_=s.P2[:, 0:512], func=AF.Silu))
                    ac = s.sig("scalar")
                    s.wait("vector", ac)
                    s.v(lambda e: e.tensor_mul(AR2[:, 2048:2560], AR2[:, 2048:2560], s.P2[:, 512:1024]))
                    s.vd()
                    s.split(AR2[:, 2048:2560], T["hh"][:], T["hl"][:])
                    ve = s.sig("vector")
                    s.wait("sync", ve)
                    hT_h = AR2[:, 1024:1280].bitcast(BF16)
                    hT_l = AR2[:, 1280:1536].bitcast(BF16)
                    thts = []
                    for c in range(4):
                        thts.append(s.dmaT("sync", hT_h[:, 128 * c:128 * c + 128], T["hh"][:, 128 * c:128 * c + 128]))
                        thts.append(s.dmaT("sync", hT_l[:, 128 * c:128 * c + 128], T["hl"][:, 128 * c:128 * c + 128]))
                    for tk in thts:
                        s.wait("tensor", tk)
                    w2h = CV["f_w2h"].rearrange("p (c n) -> p c n", c=4)
                    w2l = CV["f_w2l"].rearrange("p (c n) -> p c n", c=4)
                    for cc in range(4):
                        s.mm3(s.P1[:, 0:512], hT_h[:, 128 * cc:128 * cc + 128], hT_l[:, 128 * cc:128 * cc + 128],
                              w2h[:, cc, 0:512], w2l[:, cc, 0:512], start=(cc == 0), stop=(cc == 3))
                    for cc in range(4):
                        s.mm3(s.P1[:, 512:1024], hT_h[:, 128 * cc:128 * cc + 128], hT_l[:, 128 * cc:128 * cc + 128],
                              w2h[:, cc, 512:1024], w2l[:, cc, 512:1024], start=(cc == 0), stop=(cc == 3))
                    te = s.sig("tensor")
                    last_w2_te = te
                    s.wait("vector", te)
                    if iq == 0:
                        s.v(lambda e, ci_=ci_: e.tensor_copy(out=ysb[:, 1024 * ci_:1024 * ci_ + 1024], in_=s.P1[:, 0:1024]))
                    else:
                        s.v(lambda e, ci_=ci_: e.tensor_add(ysb[:, 1024 * ci_:1024 * ci_ + 1024],
                                                            ysb[:, 1024 * ci_:1024 * ci_ + 1024], s.P1[:, 0:1024]))
                    s.vd()
                    ve = s.sig("vector")
                    s.wait("tensor", ve)
                if iq == 3:
                    prev_ewv = None
                    for ci_, ct in enumerate(ctlist):
                        idxap = T["oidx"][:, 8 * ct:8 * ct + 8] if shared else T["fidx"][:, 8 * ct:8 * ct + 8]
                        if not shared:
                            s.wait("sync", prev_ewv)
                            tew = s.dma("sync", T["ew"][:], s.rap("cw_d", ee * CAP + 128 * ct, [[1, 128], [1, 1]]))
                            s.wait("vector", tew)
                            s.v(lambda e, ci_=ci_: e.tensor_scalar(out=ysb[:, 1024 * ci_:1024 * ci_ + 1024],
                                                                   in0=ysb[:, 1024 * ci_:1024 * ci_ + 1024],
                                                                   scalar1=T["ew"][:], scalar2=None, op0=ALU.mult))
                            s.vd()
                        ve = s.sig("vector")
                        prev_ewv = ve
                        s.wait("gpsimd", ve)
                        tys = s.gdma(lambda e, _h, ci_=ci_, idxap=idxap: e.dma_scatter_add(
                            out_ap=C["out"], in_ap=ysb[:, 1024 * ci_:1024 * ci_ + 1024].rearrange("p (n e2) -> p n e2", n=1),
                            idxs_ap=idxap, num_idxs=128, num_idxs_reg=s.regs[128],
                            elem_size=1024).then_inc(_h, 16))
                        s.wait("vector", tys)
                        s.wait("gpsimd", tys)
            s.barrier()
        for i in range(len(s.sems["dmc"])):
            for eng in ENGS:
                s.wait(eng, ("dmc", i, s.dled["dmc"][i]))
        s.barrier()

    def emit_all(s, dump=None):
        ph = "0123"
        if "0" in ph:
            s.p0()
        if "1" in ph:
            s.p1()
        if "2" in ph:
            s.p2()
        if "3" in ph:
            s.p3()
        for i in range(len(s.sems["dmc"])):
            s.wait("gpsimd", ("dmc", i, s.dled["dmc"][i]))


def run_kernel(inputs, cores=8):
    in_maps = [prep_inputs(inputs, c) for c in range(cores)]
    nc, names, C = build_nc()
    res = run_bass_kernel_spmd(nc, in_maps, core_ids=list(range(cores)))
    out = np.zeros((4, SEQ, D), np.float32)
    for c in range(cores):
        b, h = c // 2, c % 2
        o = np.asarray(res.results[c]["out"])
        tiles = [2 * si + h for si in range(NS)]
        rows = np.concatenate([np.arange(128 * T, 128 * T + 128) for T in tiles])
        out[b][rows] = o
    return out




def kernel(**inputs):
    return run_kernel(inputs)
